# version 17
# baseline (speedup 1.0000x reference)
import numpy as np
import ml_dtypes
from contextlib import ExitStack

import concourse.bass as bass
import concourse.mybir as mybir
from concourse.bass_utils import run_bass_kernel_spmd

F32 = mybir.dt.float32
BF16 = mybir.dt.bfloat16
AF = mybir.ActivationFunctionType
ALU = mybir.AluOpType
AX = mybir.AxisListType
BF = ml_dtypes.bfloat16

D = 1024
NCHUNK = 8
EXP_SHIFT = -12.0


def p2_supers(S):
    QG = S // 1024
    supers = []
    step_idx = 0
    mask_idx = 0
    for qg in range(QG):
        for ks in range(qg + 1):
            diag = ks == qg
            steps = []
            first_step = {}
            last_step = {}
            for kc in range(8):
                lo = kc * 128 if diag else 0
                subs = []
                if lo < 512:
                    subs = [(lo, 512), (512, 1024)]
                else:
                    subs = [(lo, 1024)]
                for (a, b) in subs:
                    if kc not in first_step:
                        first_step[kc] = step_idx
                    last_step[kc] = step_idx
                    steps.append((kc, a, b, step_idx))
                    step_idx += 1
            masks = {}
            if diag:
                for kc in range(8):
                    masks[kc] = mask_idx
                    mask_idx += 1
            supers.append(dict(qg=qg, ks=ks, diag=diag, steps=steps,
                               first=first_step, last=last_step, masks=masks,
                               step_end=step_idx))
    return supers


def build_nc(S=8192, n_cores=8):
    G = S // 512
    QG = S // 1024
    SUPERS = p2_supers(S)
    NSUP = len(SUPERS)

    nc = bass.Bass("TRN2", target_bir_lowering=False, debug=False)

    x_d = nc.dram_tensor("x", [S, D], F32, kind="ExternalInput").ap()
    wq_d = nc.dram_tensor("wq", [D, D], F32, kind="ExternalInput").ap()
    wk_d = nc.dram_tensor("wk", [D, D], F32, kind="ExternalInput").ap()
    wv_d = nc.dram_tensor("wv", [D, D], F32, kind="ExternalInput").ap()
    wfr_d = nc.dram_tensor("wfr", [D, D], F32, kind="ExternalInput").ap()
    wfi_d = nc.dram_tensor("wfi", [D, D], F32, kind="ExternalInput").ap()
    fb_d = nc.dram_tensor("fbias", [1, D], F32, kind="ExternalInput").ap()
    cos_d = nc.dram_tensor("cosm", [D, 128], BF16, kind="ExternalInput").ap()
    sin_d = nc.dram_tensor("sinm", [D, 128], BF16, kind="ExternalInput").ap()
    sinn_d = nc.dram_tensor("sinnm", [D, 128], BF16, kind="ExternalInput").ap()
    cc_d = nc.dram_tensor("ccm", [D, D], BF16, kind="ExternalInput").ap()
    csn_d = nc.dram_tensor("csnm", [D, D], BF16, kind="ExternalInput").ap()
    id_d = nc.dram_tensor("ident", [128, 128], BF16, kind="ExternalInput").ap()
    dm_d = nc.dram_tensor("dmask", [128, 128], BF16, kind="ExternalInput").ap()
    out_d = nc.dram_tensor("out", [S, D], F32, kind="ExternalOutput").ap()

    qT_h = nc.dram_tensor("qT_h", [8, 128, S], BF16, kind="Internal").ap()
    kT_h = nc.dram_tensor("kT_h", [8, 128, S], BF16, kind="Internal").ap()
    v_h = nc.dram_tensor("v_h", [S, D], BF16, kind="Internal").ap()
    g_h = nc.dram_tensor("g_h", [S, D], F32, kind="Internal").ap()
    xn_h = nc.dram_tensor("xn_h", [S, D], BF16, kind="Internal").ap()
    mi_h = nc.dram_tensor("mi_h", [128, D], BF16, kind="Internal")
    mf_h = nc.dram_tensor("mf_h", [n_cores * 128, D], BF16, kind="Internal")

    with ExitStack() as outer:
        sem = {}
        for name in (["dGa", "dS1", "ph0", "ph1",
                     "t_p0", "sc_p0",
                     "x0", "x1", "qS0", "kS0", "vS0", "gS",
                     "vln", "s_ln", "t_qkvl", "v_ev", "v_ev2", "sc_sil", "rv",
                     "xnst0", "xnst1", "xnl0", "xnl1",
                     "sc_done", "exp_done", "vmask", "av_done", "o_add",
                     "r_evac", "vf", "p2i",
                     "kt0", "kt1", "vv0", "vv1", "qt0", "qt1",
                     "gg0", "gg1", "gx0", "gx1", "go0", "go1"]
                     + [f"dGq{i}" for i in range(8)]
                     + [f"dGk{i}" for i in range(8)]
                     + [f"dGv{i}" for i in range(8)]
                     + ["dScos", "dSsin", "dSsnn", "vg0", "mis", "ccg", "dM"]
                     + [f"dGr{i}" for i in range(8)] + [f"dGi{i}" for i in range(8)]):
            sem[name] = outer.enter_context(nc.semaphore(name))

        def osb(name, shape, dt):
            return outer.enter_context(nc.sbuf_tensor(name, shape, dt))
        cpool = osb("cpool", [128, 4], F32)
        ones_row = osb("ones_row", [1, 128], BF16)
        ident_sb = osb("ident_sb", [128, 128], BF16)
        dmask_sb = osb("dmask_sb", [128, 128], BF16)
        mil_sb = osb("mil_sb", [128, 4], mybir.dt.int32)

        st01 = ExitStack()

        def osb01(name, shape, dt):
            return st01.enter_context(nc.sbuf_tensor(name, shape, dt))
        m_sb = osb01("m_sb", [128, 8 * D], BF16)
        bias_sb = osb01("bias_sb", [1, D], BF16)
        x_buf0 = osb01("x_buf0", [128, 4 * D], F32)
        stats = osb01("stats", [128, 32], F32)
        xc4 = [osb01(f"xc4_{i}", [128, D], F32) for i in range(4)]
        xsq_sb = osb01("xsq_sb", [128, D], F32)
        xn4 = [osb01(f"xn4_{i}", [128, D], BF16) for i in range(4)]
        xnT = [osb01(f"xnT{i}", [128, 8 * 512], BF16) for i in range(2)]
        wq_sb = osb01("wq_sb", [128, 8 * D], BF16)
        wk_sb = osb01("wk_sb", [128, 8 * D], BF16)
        wv_sb = osb01("wv_sb", [128, 8 * D], BF16)

        with ExitStack() as st0:
            def sb0(name, shape, dt):
                return st0.enter_context(nc.sbuf_tensor(name, shape, dt))
            wr_sb = sb0("wr_sb", [128, 8 * D], BF16)
            wi_sb = sb0("wi_sb", [128, 8 * D], BF16)
            trgC = sb0("trgC", [128, 8 * 128], BF16)
            trgS = sb0("trgS", [128, 8 * 128], BF16)
            trgSN = sb0("trgSN", [128, 8 * 128], BF16)
            trgCC = sb0("trgCC", [128, 8 * D], BF16)
            trgCSN = sb0("trgCSN", [128, 8 * D], BF16)
            art_sb = sb0("art_sb", [128, 8 * 128], BF16)
            ait_sb = sb0("ait_sb", [128, 8 * 128], BF16)
            mi_stage = sb0("mi_stage", [128, D], BF16)
            pp = st0.enter_context(nc.psum_tensor("pp", [128, 2048], F32))

            def p0_groups():
                gs = []
                for j in range(8):
                    gs.append(("art", j, (wr_sb, trgC), (wi_sb, trgS)))
                for j in range(8):
                    gs.append(("ait", j, (wi_sb, trgC), (wr_sb, trgSN)))
                for ub in range(2):
                    gs.append(("m", ub, (art_sb, trgCC), (ait_sb, trgCSN)))
                return gs

            with nc.Block() as blk:
                @blk.sync
                def _(sync):
                    sync.dma_start(
                        out=x_buf0[:].rearrange("p (t d) -> p t d", t=4),
                        in_=x_d[0:512, :].rearrange("(t p) d -> p t d", p=128)
                    ).then_inc(sem["x0"], 16)
                    for srcd, dst, sname in [(cos_d, trgC, "dScos"), (sin_d, trgS, "dSsin"),
                                             (sinn_d, trgSN, "dSsnn")]:
                        sync.dma_start(
                            out=dst[:].rearrange("p (c e) -> p c e", c=8),
                            in_=srcd.rearrange("(c p) e -> p c e", p=128)
                        ).then_inc(sem[sname], 16)
                    for srcd, dst in [(cc_d, trgCC), (csn_d, trgCSN)]:
                        sync.dma_start(
                            out=dst[:].rearrange("p (c e) -> p c e", c=8),
                            in_=srcd.rearrange("(c p) e -> p c e", p=128)
                        ).then_inc(sem["dS1"], 16)
                    sync.dma_start(out=ident_sb[:], in_=id_d).then_inc(sem["dS1"], 16)
                    sync.dma_start(out=dmask_sb[:], in_=dm_d).then_inc(sem["dS1"], 16)
                    for t in range(4):
                        sync.wait_ge(sem["vln"], 21 + t)
                        sync.dma_start(out=xn_h[t * 128:(t + 1) * 128, :],
                                       in_=xn4[t][:]).then_inc(sem["xnst0"], 16)
                    sync.wait_ge(sem["xnst0"], 64)
                    for c in range(8):
                        sync.dma_start_transpose(
                            xnT[0][:, c * 512:(c + 1) * 512],
                            xn_h[0:512, c * 128:(c + 1) * 128]
                        ).then_inc(sem["xnl0"], 16)
                    sync.wait_ge(sem["sc_p0"], 18)
                    sync.dma_start(out=mi_h.ap(), in_=mi_stage[:]).then_inc(sem["mis"], 16)
                    sync.wait_ge(sem["dScos"], 16)
                    sync.wait_ge(sem["dSsin"], 16)
                    sync.wait_ge(sem["dSsnn"], 16)
                    sync.wait_ge(sem["dS1"], 64)
                    sync.wait_ge(sem["mis"], 16)
                    sync.nop().then_inc(sem["ph0"], 1)

                @blk.gpsimd
                def _(gpsimd):
                    for srcd, dst, pref in [(wfr_d, wr_sb, "dGr"), (wfi_d, wi_sb, "dGi")]:
                        for j in range(8):
                            gpsimd.dma_start(
                                out=dst[:].rearrange("p (c e) -> p c e", c=8)
                                [:, :, j * 128:(j + 1) * 128],
                                in_=srcd.rearrange("(c p) e -> p c e", p=128)
                                [:, :, j * 128:(j + 1) * 128]
                            ).then_inc(sem[f"{pref}{j}"], 16)
                    for srcd, dst, pref in [(wq_d, wq_sb, "dGq"), (wk_d, wk_sb, "dGk")]:
                        for e in range(8):
                            gpsimd.dma_start(
                                out=dst[:].rearrange("p (c e) -> p c e", c=8)
                                [:, :, e * 128:(e + 1) * 128],
                                in_=srcd.rearrange("(c p) e -> p c e", p=128)
                                [:, :, e * 128:(e + 1) * 128]
                            ).then_inc(sem[f"{pref}{e}"], 16)
                    gpsimd.dma_start(out=bias_sb[:], in_=fb_d).then_inc(sem["dGa"], 16)
                    for e in range(8):
                        gpsimd.dma_start(
                            out=wv_sb[:].rearrange("p (c e) -> p c e", c=8)
                            [:, :, e * 128:(e + 1) * 128],
                            in_=wv_d.rearrange("(c p) e -> p c e", p=128)
                            [:, :, e * 128:(e + 1) * 128]
                        ).then_inc(sem[f"dGv{e}"], 16)
                    gpsimd.nop().then_inc(sem["ph0"], 1)

                @blk.vector
                def _(vector):
                    vector.memset(ones_row[:], 1.0)
                    vector.memset(cpool[:, 0:1], 1e-5)
                    vector.memset(cpool[:, 1:2], 0.0)
                    vector.memset(cpool[:, 2:3], EXP_SHIFT)
                    vcnt = 0
                    vector.wait_ge(sem["x0"], 16)
                    for t in range(4):
                        xt = x_buf0[:, t * D:(t + 1) * D]
                        c0 = 8 * t
                        if vcnt:
                            vector.wait_ge(sem["vln"], vcnt)
                        vector.reduce_sum(stats[:, c0:c0 + 1], xt, axis=AX.X)\
                            .then_inc(sem["vln"], 1); vcnt += 1
                        vector.wait_ge(sem["vln"], vcnt)
                        vector.tensor_scalar_mul(stats[:, c0 + 1:c0 + 2],
                                                 stats[:, c0:c0 + 1], -1.0 / D)\
                            .then_inc(sem["vln"], 1); vcnt += 1
                        vector.wait_ge(sem["vln"], vcnt)
                        vector.tensor_scalar_add(xc4[t][:], xt,
                                                 stats[:, c0 + 1:c0 + 2])\
                            .then_inc(sem["vln"], 1); vcnt += 1
                        vector.wait_ge(sem["vln"], vcnt)
                        vector.tensor_mul(xsq_sb[:], xc4[t][:], xc4[t][:])\
                            .then_inc(sem["vln"], 1); vcnt += 1
                        vector.wait_ge(sem["vln"], vcnt)
                        vector.reduce_sum(stats[:, c0 + 2:c0 + 3], xsq_sb[:],
                                          axis=AX.X)\
                            .then_inc(sem["vln"], 1); vcnt += 1
                    for t in range(4):
                        vector.wait_ge(sem["s_ln"], 2 * t + 2)
                        vector.wait_ge(sem["vln"], vcnt)
                        vector.tensor_scalar_mul(xn4[t][:], xc4[t][:],
                                                 stats[:, 8 * t + 4:8 * t + 5])\
                            .then_inc(sem["vln"], 1); vcnt += 1
                    vector.memset(mil_sb[:, 0:1], 0).then_inc(sem["ph0"], 1)

                @blk.tensor
                def _(tensor):
                    for gi, (kind, j, (l1, r1), (l2, r2)) in enumerate(p0_groups()):
                        if gi == 0:
                            tensor.wait_ge(sem["dScos"], 16)
                        if kind == "art":
                            tensor.wait_ge(sem[f"dGr{j}"], 16)
                        if kind == "ait":
                            tensor.wait_ge(sem[f"dGi{j}"], 16)
                            tensor.wait_ge(sem[f"dGr{j}"], 16)
                        if gi == 8:
                            tensor.wait_ge(sem["dSsnn"], 16)
                        if gi == 16:
                            tensor.wait_ge(sem["sc_p0"], 16)
                            tensor.wait_ge(sem["dS1"], 64)
                        if gi >= 4:
                            tensor.wait_ge(sem["sc_p0"], gi - 3)
                        ncols = 128 if kind != "m" else 512
                        ps = pp[:, (gi % 4) * 512:(gi % 4) * 512 + ncols]
                        for c in range(8):
                            if kind == "m":
                                lhs = l1[:, c * 128: c * 128 + 128]
                                rhs = r1[:, c * D + j * 512: c * D + j * 512 + 512]
                            else:
                                lhs = l1[:, c * D + j * 128: c * D + j * 128 + 128]
                                rhs = r1[:, c * 128:(c + 1) * 128]
                            tensor.matmul(ps, lhs, rhs, start=(c == 0), stop=False)
                        if gi == 0:
                            tensor.wait_ge(sem["dSsin"], 16)
                        if kind == "art":
                            tensor.wait_ge(sem[f"dGi{j}"], 16)
                        for c in range(8):
                            if kind == "m":
                                lhs = l2[:, c * 128: c * 128 + 128]
                                rhs = r2[:, c * D + j * 512: c * D + j * 512 + 512]
                            else:
                                lhs = l2[:, c * D + j * 128: c * D + j * 128 + 128]
                                rhs = r2[:, c * 128:(c + 1) * 128]
                            ins = tensor.matmul(ps, lhs, rhs, start=False, stop=(c == 7))
                        ins.then_inc(sem["t_p0"], 1)
                    tensor.nop().then_inc(sem["ph0"], 1)

                @blk.scalar
                def _(scalar):
                    for t in range(4):
                        c0 = 8 * t
                        scalar.wait_ge(sem["vln"], 5 * t + 5)
                        if t >= 1:
                            scalar.wait_ge(sem["s_ln"], 2 * t)
                        scalar.activation(stats[:, c0 + 3:c0 + 4],
                                          stats[:, c0 + 2:c0 + 3], AF.Ln,
                                          bias=cpool[:, 0:1], scale=1.0 / D)\
                            .then_inc(sem["s_ln"], 1)
                        scalar.wait_ge(sem["s_ln"], 2 * t + 1)
                        scalar.activation(stats[:, c0 + 4:c0 + 5],
                                          stats[:, c0 + 3:c0 + 4], AF.Exp,
                                          bias=cpool[:, 1:2], scale=-0.5)\
                            .then_inc(sem["s_ln"], 1)
                    for gi, (kind, j, _, _2) in enumerate(p0_groups()):
                        scalar.wait_ge(sem["t_p0"], gi + 1)
                        if kind == "m":
                            dst = mi_stage[:, j * 512: j * 512 + 512]
                            ps = pp[:, (gi % 4) * 512:(gi % 4) * 512 + 512]
                        else:
                            dst = (art_sb if kind == "art" else ait_sb)[:, j * 128: j * 128 + 128]
                            ps = pp[:, (gi % 4) * 512:(gi % 4) * 512 + 128]
                        scalar.copy(dst, ps).then_inc(sem["sc_p0"], 1)
                    scalar.memzero(mil_sb[:, 1:2]).then_inc(sem["ph0"], 1)

        with ExitStack() as st1:
            def sb1(name, shape, dt):
                return st1.enter_context(nc.sbuf_tensor(name, shape, dt))
            x_buf = [x_buf0, sb1("x_buf1", [128, 4 * D], F32)]
            qstage = sb1("qstage", [128, 8 * 512], BF16)
            kstage = sb1("kstage", [128, 8 * 512], BF16)
            vstage = sb1("vstage", [128, 4 * D], BF16)
            gstage = sb1("gstage", [128, 4 * D], F32)
            lin_stage = sb1("lin_stage", [128, 4 * D], F32)
            qk_ps = [st1.enter_context(nc.psum_tensor(f"qk_ps{i}", [128, 512], F32))
                     for i in range(4)]
            v_ps = st1.enter_context(nc.psum_tensor("v_ps", [128, 1024], F32))
            l_ps = st1.enter_context(nc.psum_tensor("l_ps", [128, 1024], F32))

            xsem = [sem["x0"], sem["x1"]]
            qS = sem["qS0"]
            kS = sem["kS0"]
            vS = sem["vS0"]

            def vln_val(g, t, k):
                return 24 * g + 6 * t + k

            with nc.Block() as blk:
                @blk.sync
                def _(sync):
                    sync.wait_ge(sem["ph0"], 5)
                    xnstS = [sem["xnst0"], sem["xnst1"]]
                    xnlS = [sem["xnl0"], sem["xnl1"]]
                    for g in range(G):
                        if g >= 1:
                            for t in range(4):
                                ti = 4 * g + t
                                sync.wait_ge(sem["vln"], 24 * g + 21 + t)
                                sync.dma_start(out=xn_h[ti * 128:(ti + 1) * 128, :],
                                               in_=xn4[ti % 4][:])\
                                    .then_inc(xnstS[g % 2], 16)
                        if g >= 1:
                            sync.wait_ge(xnstS[g % 2], 64 * (g // 2 + 1))
                            if g >= 2:
                                sync.wait_ge(sem["t_qkvl"], 32 * (g - 1))
                            for c in range(8):
                                sync.dma_start_transpose(
                                    xnT[g % 2][:, c * 512:(c + 1) * 512],
                                    xn_h[g * 512:(g + 1) * 512, c * 128:(c + 1) * 128]
                                ).then_inc(xnlS[g % 2], 16)
                        if g == 0 and G > 1:
                            sync.dma_start(
                                out=x_buf[1][:].rearrange("p (t d) -> p t d", t=4),
                                in_=x_d[512:1024, :].rearrange("(t p) d -> p t d", p=128)
                            ).then_inc(xsem[1], 16)
                        if g + 2 < G:
                            sync.wait_ge(sem["vln"], 24 * (g + 1))
                            sync.dma_start(
                                out=x_buf[g % 2][:].rearrange("p (t d) -> p t d", t=4),
                                in_=x_d[(g + 2) * 512:(g + 3) * 512, :]
                                .rearrange("(t p) d -> p t d", p=128)
                            ).then_inc(xsem[g % 2], 16)
                    sync.wait_ge(qS, 16 * G)
                    sync.wait_ge(kS, 16 * G)
                    sync.wait_ge(vS, 16 * G)
                    sync.wait_ge(sem["gS"], 16 * G)
                    sync.nop().then_inc(sem["ph1"], 1)

                @blk.vector
                def _(vector):
                    vector.wait_ge(sem["ph0"], 5)
                    vcnt = 24
                    for g in range(G + 1):
                        if 1 <= g < G:
                            vector.wait_ge(xsem[g % 2], 16 * (g // 2 + 1))
                            for t in range(4):
                                xt = x_buf[g % 2][:, t * D:(t + 1) * D]
                                ti = 4 * g + t
                                c0 = 8 * t
                                if g >= 1:
                                    vector.wait_ge(sem["s_ln"], 2 * (4 * (g - 1) + t) + 2)
                                if vcnt:
                                    vector.wait_ge(sem["vln"], vcnt)
                                vector.reduce_sum(stats[:, c0:c0 + 1], xt, axis=AX.X)\
                                    .then_inc(sem["vln"], 1); vcnt += 1
                                vector.wait_ge(sem["vln"], vcnt)
                                vector.tensor_scalar_mul(stats[:, c0 + 1:c0 + 2],
                                                         stats[:, c0:c0 + 1], -1.0 / D)\
                                    .then_inc(sem["vln"], 1); vcnt += 1
                                vector.wait_ge(sem["vln"], vcnt)
                                vector.tensor_scalar_add(xc4[t][:], xt,
                                                         stats[:, c0 + 1:c0 + 2])\
                                    .then_inc(sem["vln"], 1); vcnt += 1
                                vector.wait_ge(sem["vln"], vcnt)
                                vector.tensor_mul(xsq_sb[:], xc4[t][:], xc4[t][:])\
                                    .then_inc(sem["vln"], 1); vcnt += 1
                                vector.wait_ge(sem["vln"], vcnt)
                                vector.reduce_sum(stats[:, c0 + 2:c0 + 3], xsq_sb[:],
                                                  axis=AX.X)\
                                    .then_inc(sem["vln"], 1); vcnt += 1
                            for t in range(4):
                                ti = 4 * g + t
                                vector.wait_ge(sem["s_ln"], 2 * ti + 2)
                                if g >= 1:
                                    xnstS = [sem["xnst0"], sem["xnst1"]]
                                    vector.wait_ge(xnstS[(g - 1) % 2],
                                                   64 * ((g - 1) // 2 + 1))
                                vector.wait_ge(sem["vln"], vcnt)
                                vector.tensor_scalar_mul(xn4[ti % 4][:], xc4[t][:],
                                                         stats[:, 8 * t + 4:8 * t + 5])\
                                    .then_inc(sem["vln"], 1); vcnt += 1
                        if g >= 1:
                            gg = g - 1
                            for t in range(4):
                                if t == 0 and gg >= 1:
                                    vector.wait_ge(vS, 16 * gg)
                                for k, (src_ps, dst, h) in enumerate(
                                        [(v_ps, vstage, 0), (l_ps, lin_stage, 0),
                                         (v_ps, vstage, 1), (l_ps, lin_stage, 1)]):
                                    vector.wait_ge(sem["t_qkvl"],
                                                   32 * gg + 16 + 4 * t + k + 1)
                                    if k == 1 and gg >= 1:
                                        vector.wait_ge(sem["sc_sil"],
                                                       4 * (gg - 1) + t + 1)
                                    vector.tensor_copy(
                                        dst[:, t * D + h * 512: t * D + h * 512 + 512],
                                        src_ps[:, h * 512:(h + 1) * 512])\
                                        .then_inc(sem["v_ev2"], 1)
                    vector.memset(mil_sb[:, 0:1], 0).then_inc(sem["ph1"], 1)

                @blk.scalar
                def _(scalar):
                    scalar.wait_ge(sem["ph0"], 5)
                    for g in range(G + 1):
                        if g >= 1:
                            gg = g - 1
                            for j in range(16):
                                scalar.wait_ge(sem["t_qkvl"], 32 * gg + j + 1)
                                if j == 0 and gg >= 1:
                                    scalar.wait_ge(qS, 16 * gg)
                                if j == 8 and gg >= 1:
                                    scalar.wait_ge(kS, 16 * gg)
                                stg = qstage if j < 8 else kstage
                                e = j % 8
                                scalar.copy(stg[:, e * 512:(e + 1) * 512],
                                            qk_ps[j % 4][:])\
                                    .then_inc(sem["v_ev"], 1)
                            scalar.wait_ge(sem["v_ev"], 16 * gg + 16)
                            scalar.dma_start(
                                out=qT_h[:, :, gg * 512:(gg + 1) * 512]
                                .rearrange("c p s -> p c s"),
                                in_=qstage[:].rearrange("p (c s) -> p c s", c=8)
                            ).then_inc(qS, 16)
                            scalar.dma_start(
                                out=kT_h[:, :, gg * 512:(gg + 1) * 512]
                                .rearrange("c p s -> p c s"),
                                in_=kstage[:].rearrange("p (c s) -> p c s", c=8)
                            ).then_inc(kS, 16)
                        if 1 <= g < G:
                            for t in range(4):
                                ti = 4 * g + t
                                c0 = 8 * t
                                scalar.wait_ge(sem["vln"], 24 * g + 5 * t + 5)
                                if ti >= 1:
                                    scalar.wait_ge(sem["s_ln"], 2 * ti)
                                scalar.activation(stats[:, c0 + 3:c0 + 4],
                                                  stats[:, c0 + 2:c0 + 3], AF.Ln,
                                                  bias=cpool[:, 0:1], scale=1.0 / D)\
                                    .then_inc(sem["s_ln"], 1)
                                scalar.wait_ge(sem["s_ln"], 2 * ti + 1)
                                scalar.activation(stats[:, c0 + 4:c0 + 5],
                                                  stats[:, c0 + 3:c0 + 4], AF.Exp,
                                                  bias=cpool[:, 1:2], scale=-0.5)\
                                    .then_inc(sem["s_ln"], 1)
                        if g == 1:
                            scalar.wait_ge(sem["ccg"], 1)
                            scalar.dma_start(
                                out=m_sb[:].rearrange("p (c u) -> p c u", c=8),
                                in_=mf_h.ap().rearrange("(c p) u -> p c u", p=128)
                            ).then_inc(sem["dM"], 16)
                        if g >= 1:
                            gg = g - 1
                            for t in range(4):
                                scalar.wait_ge(sem["v_ev2"], 16 * gg + 4 * t + 4)
                                if t == 0 and gg >= 1:
                                    scalar.wait_ge(sem["gS"], 16 * gg)
                                scalar.activation(gstage[:, t * D:(t + 1) * D],
                                                  lin_stage[:, t * D:(t + 1) * D],
                                                  AF.Silu).then_inc(sem["sc_sil"], 1)
                            scalar.wait_ge(sem["v_ev2"], 16 * gg + 16)
                            scalar.wait_ge(sem["sc_sil"], 4 * gg + 4)
                            scalar.dma_start(
                                out=v_h[gg * 512:(gg + 1) * 512, :]
                                .rearrange("(t p) d -> p t d", p=128),
                                in_=vstage[:].rearrange("p (t d) -> p t d", t=4)
                            ).then_inc(vS, 16)
                            scalar.dma_start(
                                out=g_h[gg * 512:(gg + 1) * 512, :]
                                .rearrange("(t p) d -> p t d", p=128),
                                in_=gstage[:].rearrange("p (t d) -> p t d", t=4)
                            ).then_inc(sem["gS"], 16)
                    scalar.memzero(mil_sb[:, 1:2]).then_inc(sem["ph1"], 1)

                @blk.tensor
                def _(tensor):
                    tensor.wait_ge(sem["ph0"], 5)
                    tensor.wait_ge(sem["dGa"], 16)
                    xnlS = [sem["xnl0"], sem["xnl1"]]
                    for g in range(G):
                        tensor.wait_ge(xnlS[g % 2], 128 * (g // 2 + 1))
                        for j in range(16):
                            if g == 0:
                                if j < 8:
                                    tensor.wait_ge(sem[f"dGq{j}"], 16)
                                else:
                                    tensor.wait_ge(sem[f"dGk{j % 8}"], 16)
                            w = wq_sb if j < 8 else wk_sb
                            e = j % 8
                            qj = 16 * g + j
                            if qj >= 4:
                                tensor.wait_ge(sem["v_ev"], qj - 3)
                            for c in range(8):
                                ins = tensor.matmul(
                                    qk_ps[qj % 4][:],
                                    w[:, c * D + e * 128: c * D + e * 128 + 128],
                                    xnT[g % 2][:, c * 512:(c + 1) * 512],
                                    start=(c == 0), stop=(c == 7))
                            ins.then_inc(sem["t_qkvl"], 1)
                        if g == 0:
                            for e in range(8):
                                tensor.wait_ge(sem[f"dGv{e}"], 16)
                        for t in range(4):
                            lhs = [xnT[g % 2][:, c * 512 + t * 128: c * 512 + t * 128 + 128]
                                   for c in range(8)]
                            base = 16 * g + 4 * (t - 1)
                            for k, (dst_ps, w, h, is_lin) in enumerate(
                                    [(v_ps, wv_sb, 0, False), (l_ps, m_sb, 0, True),
                                     (v_ps, wv_sb, 1, False), (l_ps, m_sb, 1, True)]):
                                if g == 0 and t == 0 and k == 1:
                                    tensor.wait_ge(sem["dM"], 16)
                                if t > 0 or g > 0:
                                    tensor.wait_ge(sem["v_ev2"], max(0, base + k + 1))
                                for c in range(8):
                                    ins = tensor.matmul(
                                        dst_ps[:, h * 512:(h + 1) * 512], lhs[c],
                                        w[:, c * D + h * 512: c * D + (h + 1) * 512],
                                        start=(c == 0),
                                        stop=(c == 7 and not is_lin))
                                if is_lin:
                                    ins = tensor.matmul(
                                        dst_ps[:, h * 512:(h + 1) * 512],
                                        ones_row[0:1, :],
                                        bias_sb[0:1, h * 512:(h + 1) * 512],
                                        start=False, stop=True)
                                ins.then_inc(sem["t_qkvl"], 1)
                    tensor.nop().then_inc(sem["ph1"], 1)

                @blk.gpsimd
                def _(gpsimd):
                    gpsimd.wait_ge(sem["ph0"], 5)
                    gpsimd.wait_ge(sem["mis"], 16)
                    gpsimd.collective_compute(
                        "AllGather", ALU.bypass,
                        replica_groups=[list(range(n_cores))],
                        ins=[mi_h.ap().opt()],
                        outs=[mf_h.ap().opt()],
                    ).then_inc(sem["ccg"], 1)
                    gpsimd.nop().then_inc(sem["ph1"], 1)

        st01.close()
        with ExitStack() as st2:
            def sb2(name, shape, dt):
                return st2.enter_context(nc.sbuf_tensor(name, shape, dt))
            qt_buf = [sb2(f"qt_buf{i}", [128, 8 * D], BF16) for i in range(2)]
            kt_buf = [sb2(f"kt_buf{i}", [128, 8 * D], BF16) for i in range(2)]
            v_buf = [sb2(f"v_buf{i}", [128, 8 * 1032], BF16) for i in range(2)]
            pt_buf = sb2("pt_buf", [128, 8 * D], BF16)
            o_acc = [sb2(f"o_acc{i}", [128, 8 * D], F32) for i in range(2)]
            r_acc = [sb2(f"r_acc{i}", [128, 8], F32) for i in range(2)]
            rinv = [sb2(f"rinv{i}", [128, 8], F32) for i in range(2)]
            g_buf = [sb2(f"g_buf{i}", [128, D], F32) for i in range(2)]
            x2_buf = [sb2(f"x2_buf{i}", [128, D], F32) for i in range(2)]
            res_sb = sb2("res_sb", [128, D], F32)
            out_sb = [sb2(f"out_sb{i}", [128, D], F32) for i in range(2)]
            sc_ps = [st2.enter_context(nc.psum_tensor(f"sc_ps{i}", [128, 512], F32))
                     for i in range(2)]
            av_ps = [st2.enter_context(nc.psum_tensor(f"av_ps{i}", [128, 1024], F32))
                     for i in range(2)]
            r_ps = st2.enter_context(nc.psum_tensor("r_ps", [128, 8], F32))

            ktS = [sem["kt0"], sem["kt1"]]
            vvS = [sem["vv0"], sem["vv1"]]
            qtS = [sem["qt0"], sem["qt1"]]
            gS2 = [sem["gg0"], sem["gg1"]]
            xrS = [sem["gx0"], sem["gx1"]]
            obS = [sem["go0"], sem["go1"]]

            def sup_base(si):
                return 0 if si == 0 else SUPERS[si - 1]["step_end"]

            with nc.Block() as blk:
                @blk.sync
                def _(sync):
                    sync.wait_ge(sem["ph1"], 5)
                    kt_cnt = [0, 0]
                    v_cnt = [0, 0]
                    qt_cnt = [0, 0]
                    si = 0
                    for qg in range(QG):
                        qp = qg % 2
                        if qg >= 2:
                            prev_last = sum(range(1, qg)) - 1
                            sync.wait_ge(sem["sc_done"], SUPERS[prev_last]["step_end"])
                        if qg == 0:
                            qt_cnt[qp] += 1
                        else:
                            sync.dma_start(
                                out=qt_buf[qp][:].rearrange("p (c s) -> p c s", c=8),
                                in_=qT_h[:, :, qg * D:(qg + 1) * D].rearrange("c p s -> p c s")
                            ).then_inc(qtS[qp], 16)
                            qt_cnt[qp] += 1
                        for ks in range(qg + 1):
                            sp = si % 2
                            if si >= 2:
                                sync.wait_ge(sem["sc_done"], SUPERS[si - 2]["step_end"])
                            sync.dma_start(
                                out=kt_buf[sp][:].rearrange("p (c s) -> p c s", c=8),
                                in_=kT_h[:, :, ks * D:(ks + 1) * D].rearrange("c p s -> p c s")
                            ).then_inc(ktS[sp], 16)
                            kt_cnt[sp] += 1
                            if si >= 2:
                                sync.wait_ge(sem["av_done"], 8 * (si - 1))
                            if si == 0:
                                v_cnt[sp] += 1
                            else:
                                if si < 2:
                                    sync.wait_ge(sem["p2i"], 2)
                                sync.dma_start(
                                    out=v_buf[sp][:].rearrange("p (t z) -> p t z", t=8)[:, :, 0:D],
                                    in_=v_h[ks * D:(ks + 1) * D, :].rearrange("(t p) d -> p t d", p=128)
                                ).then_inc(vvS[sp], 16)
                                v_cnt[sp] += 1
                            si += 1

                @blk.tensor
                def _(tensor):
                    tensor.wait_ge(sem["ph1"], 5)
                    kt_cnt = [0, 0]
                    v_cnt = [0, 0]
                    qt_cnt = [0, 0]
                    gs = 0
                    gav = 0
                    for si, sup in enumerate(SUPERS):
                        qg, ks, diag = sup["qg"], sup["ks"], sup["diag"]
                        sp = si % 2
                        qp = qg % 2
                        kt_cnt[sp] += 1
                        tensor.wait_ge(ktS[sp], 16 * kt_cnt[sp])
                        if ks == 0:
                            qt_cnt[qp] += 1
                            tensor.wait_ge(qtS[qp], 16 * qt_cnt[qp])
                        for (kc, lo, hi, idx) in sup["steps"]:
                            if gs >= 2:
                                tensor.wait_ge(sem["exp_done"], gs - 1)
                            for c in range(8):
                                ins = tensor.matmul(
                                    sc_ps[gs % 2][:, 0:hi - lo],
                                    kt_buf[sp][:, c * D + kc * 128: c * D + kc * 128 + 128],
                                    qt_buf[qp][:, c * D + lo: c * D + hi],
                                    start=(c == 0), stop=(c == 7))
                            ins.then_inc(sem["sc_done"], 1)
                            gs += 1
                        if si == 0:
                            tensor.wait_ge(sem["vg0"], 16)
                        else:
                            v_cnt[sp] += 1
                            tensor.wait_ge(vvS[sp], 16 * v_cnt[sp])
                        tensor.wait_ge(sem["r_evac"], si)
                        waited_exp = 0
                        waited_mask = 0
                        for qt in range(8):
                            kcmax = qt if diag else 7
                            if gav >= 2:
                                tensor.wait_ge(sem["o_add"], gav - 1)
                            for kc in range(kcmax + 1):
                                need = sup["last"][kc] + 1
                                if need > waited_exp:
                                    tensor.wait_ge(sem["exp_done"], need)
                                    waited_exp = need
                                if diag and kc == qt:
                                    mneed = sup["masks"][kc] + 1
                                    if mneed > waited_mask:
                                        tensor.wait_ge(sem["vmask"], mneed)
                                        waited_mask = mneed
                                lhs = pt_buf[:, kc * D + qt * 128: kc * D + qt * 128 + 128]
                                for h in range(2):
                                    tensor.matmul(
                                        av_ps[qt % 2][:, h * 512:(h + 1) * 512], lhs,
                                        v_buf[sp][:, kc * 1032 + h * 512: kc * 1032 + (h + 1) * 512],
                                        start=(kc == 0), stop=(kc == kcmax))
                                ins = tensor.matmul(
                                    r_ps[:, qt:qt + 1], lhs,
                                    v_buf[sp][:, kc * 1032 + 1024: kc * 1032 + 1025],
                                    start=(kc == 0), stop=(kc == kcmax))
                            ins.then_inc(sem["av_done"], 1)
                            gav += 1

                @blk.scalar
                def _(scalar):
                    scalar.wait_ge(sem["ph1"], 5)
                    scalar.dma_start(
                        out=qt_buf[0][:].rearrange("p (c s) -> p c s", c=8),
                        in_=qT_h[:, :, 0:D].rearrange("c p s -> p c s")
                    ).then_inc(qtS[0], 16)
                    for si, sup in enumerate(SUPERS):
                        if si >= 1:
                            scalar.wait_ge(sem["av_done"], 8 * si)
                        for (kc, lo, hi, idx) in sup["steps"]:
                            scalar.wait_ge(sem["sc_done"], idx + 1)
                            scalar.activation(
                                pt_buf[:, kc * D + lo: kc * D + hi],
                                sc_ps[idx % 2][:, 0:hi - lo], AF.Exp,
                                bias=cpool[:, 2:3], scale=1.0 / 32.0
                            ).then_inc(sem["exp_done"], 1)

                @blk.vector
                def _(vector):
                    vector.wait_ge(sem["ph1"], 5)
                    for p in range(2):
                        vector.memset(v_buf[p][:], 1.0).then_inc(sem["p2i"], 1)
                    o_cnt = 0
                    r_cnt = 0
                    for si, sup in enumerate(SUPERS):
                        qg, ks, diag = sup["qg"], sup["ks"], sup["diag"]
                        qp = qg % 2
                        if diag:
                            for kc in range(8):
                                vector.wait_ge(sem["exp_done"], sup["first"][kc] + 1)
                                col = kc * D + kc * 128
                                vector.tensor_mul(pt_buf[:, col:col + 128],
                                                  pt_buf[:, col:col + 128],
                                                  dmask_sb[:])\
                                    .then_inc(sem["vmask"], 1)
                        for qt in range(8):
                            vector.wait_ge(sem["av_done"], 8 * si + qt + 1)
                            if o_cnt:
                                vector.wait_ge(sem["o_add"], o_cnt)
                            dst = o_acc[qp][:, qt * D:(qt + 1) * D]
                            if ks == 0:
                                if qg >= 2 and qt == 0:
                                    vector.wait_ge(sem["vf"], 24 * (qg - 1))
                                vector.tensor_copy(dst, av_ps[qt % 2][:])\
                                    .then_inc(sem["o_add"], 1)
                            else:
                                vector.tensor_add(dst, dst, av_ps[qt % 2][:])\
                                    .then_inc(sem["o_add"], 1)
                            o_cnt += 1
                        vector.wait_ge(sem["av_done"], 8 * (si + 1))
                        if r_cnt:
                            vector.wait_ge(sem["r_evac"], r_cnt)
                        if ks == 0:
                            if qg >= 2:
                                vector.wait_ge(sem["rv"], qg - 1)
                            vector.tensor_copy(r_acc[qp][:], r_ps[:, 0:8])\
                                .then_inc(sem["r_evac"], 1)
                        else:
                            vector.tensor_add(r_acc[qp][:], r_acc[qp][:], r_ps[:, 0:8])\
                                .then_inc(sem["r_evac"], 1)
                        r_cnt += 1
                        if diag:
                            vector.wait_ge(sem["r_evac"], r_cnt)
                            vector.reciprocal(rinv[qp][:], r_acc[qp][:])\
                                .then_inc(sem["rv"], 1)

                @blk.gpsimd
                def _(gpsimd):
                    gpsimd.wait_ge(sem["ph1"], 5)
                    gpsimd.wait_ge(sem["p2i"], 2)
                    gpsimd.dma_start(
                        out=v_buf[0][:].rearrange("p (t z) -> p t z", t=8)[:, :, 0:D],
                        in_=v_h[0:D, :].rearrange("(t p) d -> p t d", p=128)
                    ).then_inc(sem["vg0"], 16)
                    g_cnt = [0, 0]
                    ob_cnt = [0, 0]
                    vf_cnt = 0
                    for qg in range(QG):
                        qp = qg % 2
                        vf_base = qg * 24
                        n_adds = 8 * (qg * (qg + 1) // 2 + qg + 1)
                        for qt in range(8):
                            rows = (qg * 8 + qt) * 128
                            par = qt % 2
                            if qt < 2:
                                if g_cnt[par] > 0:
                                    gpsimd.wait_ge(sem["vf"],
                                                   vf_base - 24 + 3 * (6 + qt) + 3)
                                gpsimd.dma_start(out=g_buf[par][:],
                                                 in_=g_h[rows:rows + 128, :])\
                                    .then_inc(gS2[par], 16)
                                gpsimd.dma_start(out=x2_buf[par][:],
                                                 in_=x_d[rows:rows + 128, :])\
                                    .then_inc(xrS[par], 16)
                                g_cnt[par] += 1
                            gpsimd.wait_ge(sem["o_add"], n_adds)
                            gpsimd.wait_ge(sem["rv"], qg + 1)
                            gpsimd.wait_ge(gS2[par], 16 * g_cnt[par])
                            gpsimd.wait_ge(xrS[par], 16 * g_cnt[par])
                            if qg > 0 or qt >= 2:
                                gpsimd.wait_ge(obS[par], 16 * (qg * 4 + qt // 2))
                            if vf_cnt:
                                gpsimd.wait_ge(sem["vf"], vf_cnt)
                            gpsimd.tensor_mul(res_sb[:],
                                              o_acc[qp][:, qt * D:(qt + 1) * D],
                                              g_buf[par][:]).then_inc(sem["vf"], 1)
                            vf_cnt += 1
                            gpsimd.wait_ge(sem["vf"], vf_cnt)
                            rb = rinv[qp][:, qt:qt + 1].broadcast_to([128, D])
                            gpsimd.tensor_mul(res_sb[:], res_sb[:], rb)\
                                .then_inc(sem["vf"], 1)
                            vf_cnt += 1
                            gpsimd.wait_ge(sem["vf"], vf_cnt)
                            gpsimd.tensor_add(out_sb[par][:], res_sb[:],
                                              x2_buf[par][:]).then_inc(sem["vf"], 1)
                            vf_cnt += 1
                            gpsimd.wait_ge(sem["vf"], vf_cnt)
                            gpsimd.dma_start(out=out_d[rows:rows + 128, :],
                                             in_=out_sb[par][:])\
                                .then_inc(obS[par], 16)
                            ob_cnt[par] += 1
                            if qt + 2 < 8:
                                rows2 = (qg * 8 + qt + 2) * 128
                                gpsimd.dma_start(out=g_buf[par][:],
                                                 in_=g_h[rows2:rows2 + 128, :])\
                                    .then_inc(gS2[par], 16)
                                gpsimd.dma_start(out=x2_buf[par][:],
                                                 in_=x_d[rows2:rows2 + 128, :])\
                                    .then_inc(xrS[par], 16)
                                g_cnt[par] += 1
                    gpsimd.wait_ge(obS[0], 16 * ob_cnt[0])
                    gpsimd.wait_ge(obS[1], 16 * ob_cnt[1])

    return nc


def _consts():
    i = np.arange(D, dtype=np.float64)
    ang = 2.0 * np.pi * np.outer(i, i) / D
    ang = np.mod(ang, 2.0 * np.pi)
    cos = np.cos(ang)
    sin = np.sin(ang)
    return dict(
        cosm=cos.astype(BF),
        sinm=sin.astype(BF),
        sinnm=(-sin).astype(BF),
        ccm=(cos / D).astype(BF),
        csnm=(-sin / D).astype(BF),
        ident=np.eye(128, dtype=BF),
        dmask=(np.tril(np.ones((128, 128), np.float32)).T).astype(BF),
    )


def kernel(L0_attn_x, L0_attn_W_q, L0_attn_W_k, L0_attn_W_v,
           L0_ffn_w_freq_real, L0_ffn_w_freq_imag, L0_ffn_bias):
    S = 8192
    nc = build_nc(S)
    consts = _consts()
    shared = dict(
        wq=np.ascontiguousarray(L0_attn_W_q, dtype=np.float32),
        wk=np.ascontiguousarray(L0_attn_W_k, dtype=np.float32),
        wv=np.ascontiguousarray(L0_attn_W_v, dtype=np.float32),
        wfr=np.ascontiguousarray(L0_ffn_w_freq_real, dtype=np.float32),
        wfi=np.ascontiguousarray(L0_ffn_w_freq_imag, dtype=np.float32),
        fbias=np.ascontiguousarray(L0_ffn_bias, dtype=np.float32).reshape(1, D),
        **consts,
    )
    x = np.asarray(L0_attn_x, dtype=np.float32)
    in_maps = []
    for b in range(8):
        m = dict(x=np.ascontiguousarray(x[b]), **shared)
        for nm in ("cosm", "sinm", "sinnm"):
            m[nm] = np.ascontiguousarray(shared[nm][:, b * 128:(b + 1) * 128])
        in_maps.append(m)
    res = run_bass_kernel_spmd(nc, in_maps, core_ids=list(range(8)))
    out = np.stack([np.asarray(res.results[b]["out"]) for b in range(8)], axis=0)
    return out.astype(np.float32)


# version 18
# speedup vs baseline: 1.0111x; 1.0111x over previous
import numpy as np
import ml_dtypes
from contextlib import ExitStack

import concourse.bass as bass
import concourse.mybir as mybir
from concourse.bass_utils import run_bass_kernel_spmd

F32 = mybir.dt.float32
BF16 = mybir.dt.bfloat16
AF = mybir.ActivationFunctionType
ALU = mybir.AluOpType
AX = mybir.AxisListType
BF = ml_dtypes.bfloat16

D = 1024
NCHUNK = 8
EXP_SHIFT = -12.0


def p2_supers(S):
    QG = S // 1024
    supers = []
    step_idx = 0
    mask_idx = 0
    for qg in range(QG):
        for ks in range(qg + 1):
            diag = ks == qg
            steps = []
            first_step = {}
            last_step = {}
            for kc in range(8):
                lo = kc * 128 if diag else 0
                subs = []
                if lo < 512:
                    subs = [(lo, 512), (512, 1024)]
                else:
                    subs = [(lo, 1024)]
                for (a, b) in subs:
                    if kc not in first_step:
                        first_step[kc] = step_idx
                    last_step[kc] = step_idx
                    steps.append((kc, a, b, step_idx))
                    step_idx += 1
            masks = {}
            if diag:
                for kc in range(8):
                    masks[kc] = mask_idx
                    mask_idx += 1
            supers.append(dict(qg=qg, ks=ks, diag=diag, steps=steps,
                               first=first_step, last=last_step, masks=masks,
                               step_end=step_idx))
    return supers


def build_nc(S=8192, n_cores=8):
    G = S // 512
    QG = S // 1024
    SUPERS = p2_supers(S)
    NSUP = len(SUPERS)

    nc = bass.Bass("TRN2", target_bir_lowering=False, debug=False)

    x_d = nc.dram_tensor("x", [S, D], F32, kind="ExternalInput").ap()
    wq_d = nc.dram_tensor("wq", [D, D], F32, kind="ExternalInput").ap()
    wk_d = nc.dram_tensor("wk", [D, D], F32, kind="ExternalInput").ap()
    wv_d = nc.dram_tensor("wv", [D, D], F32, kind="ExternalInput").ap()
    wfr_d = nc.dram_tensor("wfr", [D, D], F32, kind="ExternalInput").ap()
    wfi_d = nc.dram_tensor("wfi", [D, D], F32, kind="ExternalInput").ap()
    fb_d = nc.dram_tensor("fbias", [1, D], F32, kind="ExternalInput").ap()
    cos_d = nc.dram_tensor("cosm", [D, 128], BF16, kind="ExternalInput").ap()
    sin_d = nc.dram_tensor("sinm", [D, 128], BF16, kind="ExternalInput").ap()
    sinn_d = nc.dram_tensor("sinnm", [D, 128], BF16, kind="ExternalInput").ap()
    cc_d = nc.dram_tensor("ccm", [D, D], BF16, kind="ExternalInput").ap()
    csn_d = nc.dram_tensor("csnm", [D, D], BF16, kind="ExternalInput").ap()
    id_d = nc.dram_tensor("ident", [128, 128], BF16, kind="ExternalInput").ap()
    dm_d = nc.dram_tensor("dmask", [128, 128], BF16, kind="ExternalInput").ap()
    out_d = nc.dram_tensor("out", [S, D], F32, kind="ExternalOutput").ap()

    qT_h = nc.dram_tensor("qT_h", [8, 128, S], BF16, kind="Internal").ap()
    kT_h = nc.dram_tensor("kT_h", [8, 128, S], BF16, kind="Internal").ap()
    v_h = nc.dram_tensor("v_h", [S, D], BF16, kind="Internal").ap()
    g_h = nc.dram_tensor("g_h", [S, D], F32, kind="Internal").ap()
    xn_h = nc.dram_tensor("xn_h", [S, D], BF16, kind="Internal").ap()
    mi_h = nc.dram_tensor("mi_h", [128, D], BF16, kind="Internal")
    mf_h = nc.dram_tensor("mf_h", [n_cores * 128, D], BF16, kind="Internal")

    with ExitStack() as outer:
        sem = {}
        for name in (["dGa", "dS1", "ph0", "ph1",
                     "t_p0", "sc_p0",
                     "x0", "x1", "qS0", "kS0", "vS0", "gS",
                     "vln", "s_ln", "t_qkvl", "v_ev", "v_ev2", "sc_sil", "rv",
                     "xnst0", "xnst1", "xnl0", "xnl1",
                     "sc_done", "exp_done", "vmask", "av_done", "o_add",
                     "r_evac", "vf", "p2i",
                     "kt0", "kt1", "vv0", "vv1", "qt0", "qt1",
                     "gg0", "gg1", "gx0", "gx1", "go0", "go1"]
                     + [f"dGq{i}" for i in range(8)]
                     + [f"dGk{i}" for i in range(8)]
                     + [f"dGv{i}" for i in range(8)]
                     + ["dScos", "dSsin", "dSsnn", "vg0", "mis", "ccg", "dM", "vf2"]
                     + [f"dGr{i}" for i in range(8)] + [f"dGi{i}" for i in range(8)]):
            sem[name] = outer.enter_context(nc.semaphore(name))

        def osb(name, shape, dt):
            return outer.enter_context(nc.sbuf_tensor(name, shape, dt))
        cpool = osb("cpool", [128, 4], F32)
        ones_row = osb("ones_row", [1, 128], BF16)
        ident_sb = osb("ident_sb", [128, 128], BF16)
        dmask_sb = osb("dmask_sb", [128, 128], BF16)
        mil_sb = osb("mil_sb", [128, 4], mybir.dt.int32)

        st01 = ExitStack()

        def osb01(name, shape, dt):
            return st01.enter_context(nc.sbuf_tensor(name, shape, dt))
        m_sb = osb01("m_sb", [128, 8 * D], BF16)
        bias_sb = osb01("bias_sb", [1, D], BF16)
        x_buf0 = osb01("x_buf0", [128, 4 * D], F32)
        stats = osb01("stats", [128, 32], F32)
        xc4 = [osb01(f"xc4_{i}", [128, D], F32) for i in range(4)]
        xsq_sb = osb01("xsq_sb", [128, D], F32)
        xn4 = [osb01(f"xn4_{i}", [128, D], BF16) for i in range(4)]
        xnT = [osb01(f"xnT{i}", [128, 8 * 512], BF16) for i in range(2)]
        wq_sb = osb01("wq_sb", [128, 8 * D], BF16)
        wk_sb = osb01("wk_sb", [128, 8 * D], BF16)
        wv_sb = osb01("wv_sb", [128, 8 * D], BF16)

        with ExitStack() as st0:
            def sb0(name, shape, dt):
                return st0.enter_context(nc.sbuf_tensor(name, shape, dt))
            wr_sb = sb0("wr_sb", [128, 8 * D], BF16)
            wi_sb = sb0("wi_sb", [128, 8 * D], BF16)
            trgC = sb0("trgC", [128, 8 * 128], BF16)
            trgS = sb0("trgS", [128, 8 * 128], BF16)
            trgSN = sb0("trgSN", [128, 8 * 128], BF16)
            trgCC = sb0("trgCC", [128, 8 * D], BF16)
            trgCSN = sb0("trgCSN", [128, 8 * D], BF16)
            art_sb = sb0("art_sb", [128, 8 * 128], BF16)
            ait_sb = sb0("ait_sb", [128, 8 * 128], BF16)
            mi_stage = sb0("mi_stage", [128, D], BF16)
            pp = st0.enter_context(nc.psum_tensor("pp", [128, 2048], F32))

            def p0_groups():
                gs = []
                for j in range(8):
                    gs.append(("art", j, (wr_sb, trgC), (wi_sb, trgS)))
                for j in range(8):
                    gs.append(("ait", j, (wi_sb, trgC), (wr_sb, trgSN)))
                for ub in range(2):
                    gs.append(("m", ub, (art_sb, trgCC), (ait_sb, trgCSN)))
                return gs

            with nc.Block() as blk:
                @blk.sync
                def _(sync):
                    sync.dma_start(
                        out=x_buf0[:].rearrange("p (t d) -> p t d", t=4),
                        in_=x_d[0:512, :].rearrange("(t p) d -> p t d", p=128)
                    ).then_inc(sem["x0"], 16)
                    for srcd, dst, sname in [(cos_d, trgC, "dScos"), (sin_d, trgS, "dSsin"),
                                             (sinn_d, trgSN, "dSsnn")]:
                        sync.dma_start(
                            out=dst[:].rearrange("p (c e) -> p c e", c=8),
                            in_=srcd.rearrange("(c p) e -> p c e", p=128)
                        ).then_inc(sem[sname], 16)
                    for srcd, dst in [(cc_d, trgCC), (csn_d, trgCSN)]:
                        sync.dma_start(
                            out=dst[:].rearrange("p (c e) -> p c e", c=8),
                            in_=srcd.rearrange("(c p) e -> p c e", p=128)
                        ).then_inc(sem["dS1"], 16)
                    sync.dma_start(out=ident_sb[:], in_=id_d).then_inc(sem["dS1"], 16)
                    sync.dma_start(out=dmask_sb[:], in_=dm_d).then_inc(sem["dS1"], 16)
                    for t in range(4):
                        sync.wait_ge(sem["vln"], 21 + t)
                        sync.dma_start(out=xn_h[t * 128:(t + 1) * 128, :],
                                       in_=xn4[t][:]).then_inc(sem["xnst0"], 16)
                    sync.wait_ge(sem["xnst0"], 64)
                    for c in range(8):
                        sync.dma_start_transpose(
                            xnT[0][:, c * 512:(c + 1) * 512],
                            xn_h[0:512, c * 128:(c + 1) * 128]
                        ).then_inc(sem["xnl0"], 16)
                    sync.wait_ge(sem["sc_p0"], 18)
                    sync.dma_start(out=mi_h.ap(), in_=mi_stage[:]).then_inc(sem["mis"], 16)
                    sync.wait_ge(sem["dScos"], 16)
                    sync.wait_ge(sem["dSsin"], 16)
                    sync.wait_ge(sem["dSsnn"], 16)
                    sync.wait_ge(sem["dS1"], 64)
                    sync.wait_ge(sem["mis"], 16)
                    sync.nop().then_inc(sem["ph0"], 1)

                @blk.gpsimd
                def _(gpsimd):
                    for srcd, dst, pref in [(wfr_d, wr_sb, "dGr"), (wfi_d, wi_sb, "dGi")]:
                        for j in range(8):
                            gpsimd.dma_start(
                                out=dst[:].rearrange("p (c e) -> p c e", c=8)
                                [:, :, j * 128:(j + 1) * 128],
                                in_=srcd.rearrange("(c p) e -> p c e", p=128)
                                [:, :, j * 128:(j + 1) * 128]
                            ).then_inc(sem[f"{pref}{j}"], 16)
                    for srcd, dst, pref in [(wq_d, wq_sb, "dGq"), (wk_d, wk_sb, "dGk")]:
                        for e in range(8):
                            gpsimd.dma_start(
                                out=dst[:].rearrange("p (c e) -> p c e", c=8)
                                [:, :, e * 128:(e + 1) * 128],
                                in_=srcd.rearrange("(c p) e -> p c e", p=128)
                                [:, :, e * 128:(e + 1) * 128]
                            ).then_inc(sem[f"{pref}{e}"], 16)
                    gpsimd.dma_start(out=bias_sb[:], in_=fb_d).then_inc(sem["dGa"], 16)
                    for e in range(8):
                        gpsimd.dma_start(
                            out=wv_sb[:].rearrange("p (c e) -> p c e", c=8)
                            [:, :, e * 128:(e + 1) * 128],
                            in_=wv_d.rearrange("(c p) e -> p c e", p=128)
                            [:, :, e * 128:(e + 1) * 128]
                        ).then_inc(sem[f"dGv{e}"], 16)
                    gpsimd.nop().then_inc(sem["ph0"], 1)

                @blk.vector
                def _(vector):
                    vector.memset(ones_row[:], 1.0)
                    vector.memset(cpool[:, 0:1], 1e-5)
                    vector.memset(cpool[:, 1:2], 0.0)
                    vector.memset(cpool[:, 2:3], EXP_SHIFT)
                    vcnt = 0
                    vector.wait_ge(sem["x0"], 16)
                    for t in range(4):
                        xt = x_buf0[:, t * D:(t + 1) * D]
                        c0 = 8 * t
                        if vcnt:
                            vector.wait_ge(sem["vln"], vcnt)
                        vector.reduce_sum(stats[:, c0:c0 + 1], xt, axis=AX.X)\
                            .then_inc(sem["vln"], 1); vcnt += 1
                        vector.wait_ge(sem["vln"], vcnt)
                        vector.tensor_scalar_mul(stats[:, c0 + 1:c0 + 2],
                                                 stats[:, c0:c0 + 1], -1.0 / D)\
                            .then_inc(sem["vln"], 1); vcnt += 1
                        vector.wait_ge(sem["vln"], vcnt)
                        vector.tensor_scalar_add(xc4[t][:], xt,
                                                 stats[:, c0 + 1:c0 + 2])\
                            .then_inc(sem["vln"], 1); vcnt += 1
                        vector.wait_ge(sem["vln"], vcnt)
                        vector.tensor_mul(xsq_sb[:], xc4[t][:], xc4[t][:])\
                            .then_inc(sem["vln"], 1); vcnt += 1
                        vector.wait_ge(sem["vln"], vcnt)
                        vector.reduce_sum(stats[:, c0 + 2:c0 + 3], xsq_sb[:],
                                          axis=AX.X)\
                            .then_inc(sem["vln"], 1); vcnt += 1
                    for t in range(4):
                        vector.wait_ge(sem["s_ln"], 2 * t + 2)
                        vector.wait_ge(sem["vln"], vcnt)
                        vector.tensor_scalar_mul(xn4[t][:], xc4[t][:],
                                                 stats[:, 8 * t + 4:8 * t + 5])\
                            .then_inc(sem["vln"], 1); vcnt += 1
                    vector.memset(mil_sb[:, 0:1], 0).then_inc(sem["ph0"], 1)

                @blk.tensor
                def _(tensor):
                    for gi, (kind, j, (l1, r1), (l2, r2)) in enumerate(p0_groups()):
                        if gi == 0:
                            tensor.wait_ge(sem["dScos"], 16)
                        if kind == "art":
                            tensor.wait_ge(sem[f"dGr{j}"], 16)
                        if kind == "ait":
                            tensor.wait_ge(sem[f"dGi{j}"], 16)
                            tensor.wait_ge(sem[f"dGr{j}"], 16)
                        if gi == 8:
                            tensor.wait_ge(sem["dSsnn"], 16)
                        if gi == 16:
                            tensor.wait_ge(sem["sc_p0"], 16)
                            tensor.wait_ge(sem["dS1"], 64)
                        if gi >= 4:
                            tensor.wait_ge(sem["sc_p0"], gi - 3)
                        ncols = 128 if kind != "m" else 512
                        ps = pp[:, (gi % 4) * 512:(gi % 4) * 512 + ncols]
                        for c in range(8):
                            if kind == "m":
                                lhs = l1[:, c * 128: c * 128 + 128]
                                rhs = r1[:, c * D + j * 512: c * D + j * 512 + 512]
                            else:
                                lhs = l1[:, c * D + j * 128: c * D + j * 128 + 128]
                                rhs = r1[:, c * 128:(c + 1) * 128]
                            tensor.matmul(ps, lhs, rhs, start=(c == 0), stop=False)
                        if gi == 0:
                            tensor.wait_ge(sem["dSsin"], 16)
                        if kind == "art":
                            tensor.wait_ge(sem[f"dGi{j}"], 16)
                        for c in range(8):
                            if kind == "m":
                                lhs = l2[:, c * 128: c * 128 + 128]
                                rhs = r2[:, c * D + j * 512: c * D + j * 512 + 512]
                            else:
                                lhs = l2[:, c * D + j * 128: c * D + j * 128 + 128]
                                rhs = r2[:, c * 128:(c + 1) * 128]
                            ins = tensor.matmul(ps, lhs, rhs, start=False, stop=(c == 7))
                        ins.then_inc(sem["t_p0"], 1)
                    tensor.nop().then_inc(sem["ph0"], 1)

                @blk.scalar
                def _(scalar):
                    for t in range(4):
                        c0 = 8 * t
                        scalar.wait_ge(sem["vln"], 5 * t + 5)
                        if t >= 1:
                            scalar.wait_ge(sem["s_ln"], 2 * t)
                        scalar.activation(stats[:, c0 + 3:c0 + 4],
                                          stats[:, c0 + 2:c0 + 3], AF.Ln,
                                          bias=cpool[:, 0:1], scale=1.0 / D)\
                            .then_inc(sem["s_ln"], 1)
                        scalar.wait_ge(sem["s_ln"], 2 * t + 1)
                        scalar.activation(stats[:, c0 + 4:c0 + 5],
                                          stats[:, c0 + 3:c0 + 4], AF.Exp,
                                          bias=cpool[:, 1:2], scale=-0.5)\
                            .then_inc(sem["s_ln"], 1)
                    for gi, (kind, j, _, _2) in enumerate(p0_groups()):
                        scalar.wait_ge(sem["t_p0"], gi + 1)
                        if kind == "m":
                            dst = mi_stage[:, j * 512: j * 512 + 512]
                            ps = pp[:, (gi % 4) * 512:(gi % 4) * 512 + 512]
                        else:
                            dst = (art_sb if kind == "art" else ait_sb)[:, j * 128: j * 128 + 128]
                            ps = pp[:, (gi % 4) * 512:(gi % 4) * 512 + 128]
                        scalar.copy(dst, ps).then_inc(sem["sc_p0"], 1)
                    scalar.memzero(mil_sb[:, 1:2]).then_inc(sem["ph0"], 1)

        with ExitStack() as st1:
            def sb1(name, shape, dt):
                return st1.enter_context(nc.sbuf_tensor(name, shape, dt))
            x_buf = [x_buf0, sb1("x_buf1", [128, 4 * D], F32)]
            qstage = sb1("qstage", [128, 8 * 512], BF16)
            kstage = sb1("kstage", [128, 8 * 512], BF16)
            vstage = sb1("vstage", [128, 4 * D], BF16)
            gstage = sb1("gstage", [128, 4 * D], F32)
            lin_stage = sb1("lin_stage", [128, 4 * D], F32)
            qk_ps = [st1.enter_context(nc.psum_tensor(f"qk_ps{i}", [128, 512], F32))
                     for i in range(4)]
            v_ps = st1.enter_context(nc.psum_tensor("v_ps", [128, 1024], F32))
            l_ps = st1.enter_context(nc.psum_tensor("l_ps", [128, 1024], F32))

            xsem = [sem["x0"], sem["x1"]]
            qS = sem["qS0"]
            kS = sem["kS0"]
            vS = sem["vS0"]

            def vln_val(g, t, k):
                return 24 * g + 6 * t + k

            with nc.Block() as blk:
                @blk.sync
                def _(sync):
                    sync.wait_ge(sem["ph0"], 5)
                    xnstS = [sem["xnst0"], sem["xnst1"]]
                    xnlS = [sem["xnl0"], sem["xnl1"]]
                    for g in range(G):
                        if g >= 1:
                            for t in range(4):
                                ti = 4 * g + t
                                sync.wait_ge(sem["vln"], 24 * g + 21 + t)
                                sync.dma_start(out=xn_h[ti * 128:(ti + 1) * 128, :],
                                               in_=xn4[ti % 4][:])\
                                    .then_inc(xnstS[g % 2], 16)
                        if g >= 1:
                            sync.wait_ge(xnstS[g % 2], 64 * (g // 2 + 1))
                            if g >= 2:
                                sync.wait_ge(sem["t_qkvl"], 32 * (g - 1))
                            for c in range(8):
                                sync.dma_start_transpose(
                                    xnT[g % 2][:, c * 512:(c + 1) * 512],
                                    xn_h[g * 512:(g + 1) * 512, c * 128:(c + 1) * 128]
                                ).then_inc(xnlS[g % 2], 16)
                        if g == 0 and G > 1:
                            sync.dma_start(
                                out=x_buf[1][:].rearrange("p (t d) -> p t d", t=4),
                                in_=x_d[512:1024, :].rearrange("(t p) d -> p t d", p=128)
                            ).then_inc(xsem[1], 16)
                        if g + 2 < G:
                            sync.wait_ge(sem["vln"], 24 * (g + 1))
                            sync.dma_start(
                                out=x_buf[g % 2][:].rearrange("p (t d) -> p t d", t=4),
                                in_=x_d[(g + 2) * 512:(g + 3) * 512, :]
                                .rearrange("(t p) d -> p t d", p=128)
                            ).then_inc(xsem[g % 2], 16)
                    sync.wait_ge(qS, 16 * G)
                    sync.wait_ge(kS, 16 * G)
                    sync.wait_ge(vS, 16 * G)
                    sync.wait_ge(sem["gS"], 16 * G)
                    sync.nop().then_inc(sem["ph1"], 1)

                @blk.vector
                def _(vector):
                    vector.wait_ge(sem["ph0"], 5)
                    vcnt = 24
                    for g in range(G + 1):
                        if 1 <= g < G:
                            vector.wait_ge(xsem[g % 2], 16 * (g // 2 + 1))
                            for t in range(4):
                                xt = x_buf[g % 2][:, t * D:(t + 1) * D]
                                ti = 4 * g + t
                                c0 = 8 * t
                                if g >= 1:
                                    vector.wait_ge(sem["s_ln"], 2 * (4 * (g - 1) + t) + 2)
                                if vcnt:
                                    vector.wait_ge(sem["vln"], vcnt)
                                vector.reduce_sum(stats[:, c0:c0 + 1], xt, axis=AX.X)\
                                    .then_inc(sem["vln"], 1); vcnt += 1
                                vector.wait_ge(sem["vln"], vcnt)
                                vector.tensor_scalar_mul(stats[:, c0 + 1:c0 + 2],
                                                         stats[:, c0:c0 + 1], -1.0 / D)\
                                    .then_inc(sem["vln"], 1); vcnt += 1
                                vector.wait_ge(sem["vln"], vcnt)
                                vector.tensor_scalar_add(xc4[t][:], xt,
                                                         stats[:, c0 + 1:c0 + 2])\
                                    .then_inc(sem["vln"], 1); vcnt += 1
                                vector.wait_ge(sem["vln"], vcnt)
                                vector.tensor_mul(xsq_sb[:], xc4[t][:], xc4[t][:])\
                                    .then_inc(sem["vln"], 1); vcnt += 1
                                vector.wait_ge(sem["vln"], vcnt)
                                vector.reduce_sum(stats[:, c0 + 2:c0 + 3], xsq_sb[:],
                                                  axis=AX.X)\
                                    .then_inc(sem["vln"], 1); vcnt += 1
                            for t in range(4):
                                ti = 4 * g + t
                                vector.wait_ge(sem["s_ln"], 2 * ti + 2)
                                if g >= 1:
                                    xnstS = [sem["xnst0"], sem["xnst1"]]
                                    vector.wait_ge(xnstS[(g - 1) % 2],
                                                   64 * ((g - 1) // 2 + 1))
                                vector.wait_ge(sem["vln"], vcnt)
                                vector.tensor_scalar_mul(xn4[ti % 4][:], xc4[t][:],
                                                         stats[:, 8 * t + 4:8 * t + 5])\
                                    .then_inc(sem["vln"], 1); vcnt += 1
                        if g >= 1:
                            gg = g - 1
                            for t in range(4):
                                if t == 0 and gg >= 1:
                                    vector.wait_ge(vS, 16 * gg)
                                for k, (src_ps, dst, h) in enumerate(
                                        [(v_ps, vstage, 0), (l_ps, lin_stage, 0),
                                         (v_ps, vstage, 1), (l_ps, lin_stage, 1)]):
                                    vector.wait_ge(sem["t_qkvl"],
                                                   32 * gg + 16 + 4 * t + k + 1)
                                    if k == 1 and gg >= 1:
                                        vector.wait_ge(sem["sc_sil"],
                                                       4 * (gg - 1) + t + 1)
                                    vector.tensor_copy(
                                        dst[:, t * D + h * 512: t * D + h * 512 + 512],
                                        src_ps[:, h * 512:(h + 1) * 512])\
                                        .then_inc(sem["v_ev2"], 1)
                    vector.memset(mil_sb[:, 0:1], 0).then_inc(sem["ph1"], 1)

                @blk.scalar
                def _(scalar):
                    scalar.wait_ge(sem["ph0"], 5)
                    for g in range(G + 1):
                        if g >= 1:
                            gg = g - 1
                            for j in range(16):
                                scalar.wait_ge(sem["t_qkvl"], 32 * gg + j + 1)
                                if j == 0 and gg >= 1:
                                    scalar.wait_ge(qS, 16 * gg)
                                if j == 8 and gg >= 1:
                                    scalar.wait_ge(kS, 16 * gg)
                                stg = qstage if j < 8 else kstage
                                e = j % 8
                                scalar.copy(stg[:, e * 512:(e + 1) * 512],
                                            qk_ps[j % 4][:])\
                                    .then_inc(sem["v_ev"], 1)
                            scalar.wait_ge(sem["v_ev"], 16 * gg + 16)
                            scalar.dma_start(
                                out=qT_h[:, :, gg * 512:(gg + 1) * 512]
                                .rearrange("c p s -> p c s"),
                                in_=qstage[:].rearrange("p (c s) -> p c s", c=8)
                            ).then_inc(qS, 16)
                            scalar.dma_start(
                                out=kT_h[:, :, gg * 512:(gg + 1) * 512]
                                .rearrange("c p s -> p c s"),
                                in_=kstage[:].rearrange("p (c s) -> p c s", c=8)
                            ).then_inc(kS, 16)
                        if 1 <= g < G:
                            for t in range(4):
                                ti = 4 * g + t
                                c0 = 8 * t
                                scalar.wait_ge(sem["vln"], 24 * g + 5 * t + 5)
                                if ti >= 1:
                                    scalar.wait_ge(sem["s_ln"], 2 * ti)
                                scalar.activation(stats[:, c0 + 3:c0 + 4],
                                                  stats[:, c0 + 2:c0 + 3], AF.Ln,
                                                  bias=cpool[:, 0:1], scale=1.0 / D)\
                                    .then_inc(sem["s_ln"], 1)
                                scalar.wait_ge(sem["s_ln"], 2 * ti + 1)
                                scalar.activation(stats[:, c0 + 4:c0 + 5],
                                                  stats[:, c0 + 3:c0 + 4], AF.Exp,
                                                  bias=cpool[:, 1:2], scale=-0.5)\
                                    .then_inc(sem["s_ln"], 1)
                        if g == 1:
                            scalar.wait_ge(sem["ccg"], 1)
                            scalar.dma_start(
                                out=m_sb[:].rearrange("p (c u) -> p c u", c=8),
                                in_=mf_h.ap().rearrange("(c p) u -> p c u", p=128)
                            ).then_inc(sem["dM"], 16)
                        if g >= 1:
                            gg = g - 1
                            for t in range(4):
                                scalar.wait_ge(sem["v_ev2"], 16 * gg + 4 * t + 4)
                                if t == 0 and gg >= 1:
                                    scalar.wait_ge(sem["gS"], 16 * gg)
                                scalar.activation(gstage[:, t * D:(t + 1) * D],
                                                  lin_stage[:, t * D:(t + 1) * D],
                                                  AF.Silu).then_inc(sem["sc_sil"], 1)
                            scalar.wait_ge(sem["v_ev2"], 16 * gg + 16)
                            scalar.wait_ge(sem["sc_sil"], 4 * gg + 4)
                            scalar.dma_start(
                                out=v_h[gg * 512:(gg + 1) * 512, :]
                                .rearrange("(t p) d -> p t d", p=128),
                                in_=vstage[:].rearrange("p (t d) -> p t d", t=4)
                            ).then_inc(vS, 16)
                            scalar.dma_start(
                                out=g_h[gg * 512:(gg + 1) * 512, :]
                                .rearrange("(t p) d -> p t d", p=128),
                                in_=gstage[:].rearrange("p (t d) -> p t d", t=4)
                            ).then_inc(sem["gS"], 16)
                    scalar.memzero(mil_sb[:, 1:2]).then_inc(sem["ph1"], 1)

                @blk.tensor
                def _(tensor):
                    tensor.wait_ge(sem["ph0"], 5)
                    tensor.wait_ge(sem["dGa"], 16)
                    xnlS = [sem["xnl0"], sem["xnl1"]]
                    for g in range(G):
                        tensor.wait_ge(xnlS[g % 2], 128 * (g // 2 + 1))
                        for j in range(16):
                            if g == 0:
                                if j < 8:
                                    tensor.wait_ge(sem[f"dGq{j}"], 16)
                                else:
                                    tensor.wait_ge(sem[f"dGk{j % 8}"], 16)
                            w = wq_sb if j < 8 else wk_sb
                            e = j % 8
                            qj = 16 * g + j
                            if qj >= 4:
                                tensor.wait_ge(sem["v_ev"], qj - 3)
                            for c in range(8):
                                ins = tensor.matmul(
                                    qk_ps[qj % 4][:],
                                    w[:, c * D + e * 128: c * D + e * 128 + 128],
                                    xnT[g % 2][:, c * 512:(c + 1) * 512],
                                    start=(c == 0), stop=(c == 7))
                            ins.then_inc(sem["t_qkvl"], 1)
                        if g == 0:
                            for e in range(8):
                                tensor.wait_ge(sem[f"dGv{e}"], 16)
                        for t in range(4):
                            lhs = [xnT[g % 2][:, c * 512 + t * 128: c * 512 + t * 128 + 128]
                                   for c in range(8)]
                            base = 16 * g + 4 * (t - 1)
                            for k, (dst_ps, w, h, is_lin) in enumerate(
                                    [(v_ps, wv_sb, 0, False), (l_ps, m_sb, 0, True),
                                     (v_ps, wv_sb, 1, False), (l_ps, m_sb, 1, True)]):
                                if g == 0 and t == 0 and k == 1:
                                    tensor.wait_ge(sem["dM"], 16)
                                if t > 0 or g > 0:
                                    tensor.wait_ge(sem["v_ev2"], max(0, base + k + 1))
                                for c in range(8):
                                    ins = tensor.matmul(
                                        dst_ps[:, h * 512:(h + 1) * 512], lhs[c],
                                        w[:, c * D + h * 512: c * D + (h + 1) * 512],
                                        start=(c == 0),
                                        stop=(c == 7 and not is_lin))
                                if is_lin:
                                    ins = tensor.matmul(
                                        dst_ps[:, h * 512:(h + 1) * 512],
                                        ones_row[0:1, :],
                                        bias_sb[0:1, h * 512:(h + 1) * 512],
                                        start=False, stop=True)
                                ins.then_inc(sem["t_qkvl"], 1)
                    tensor.nop().then_inc(sem["ph1"], 1)

                @blk.gpsimd
                def _(gpsimd):
                    gpsimd.wait_ge(sem["ph0"], 5)
                    gpsimd.wait_ge(sem["mis"], 16)
                    gpsimd.collective_compute(
                        "AllGather", ALU.bypass,
                        replica_groups=[list(range(n_cores))],
                        ins=[mi_h.ap().opt()],
                        outs=[mf_h.ap().opt()],
                    ).then_inc(sem["ccg"], 1)
                    gpsimd.nop().then_inc(sem["ph1"], 1)

        st01.close()
        with ExitStack() as st2:
            def sb2(name, shape, dt):
                return st2.enter_context(nc.sbuf_tensor(name, shape, dt))
            qt_buf = [sb2(f"qt_buf{i}", [128, 8 * D], BF16) for i in range(2)]
            kt_buf = [sb2(f"kt_buf{i}", [128, 8 * D], BF16) for i in range(2)]
            v_buf = [sb2(f"v_buf{i}", [128, 8 * 1032], BF16) for i in range(2)]
            pt_buf = sb2("pt_buf", [128, 8 * D], BF16)
            o_acc = [sb2(f"o_acc{i}", [128, 8 * D], F32) for i in range(2)]
            r_acc = [sb2(f"r_acc{i}", [128, 8], F32) for i in range(2)]
            rinv = [sb2(f"rinv{i}", [128, 8], F32) for i in range(2)]
            g_buf = [sb2(f"g_buf{i}", [128, D], F32) for i in range(2)]
            x2_buf = [sb2(f"x2_buf{i}", [128, D], F32) for i in range(2)]
            res_sb = sb2("res_sb", [128, D], F32)
            out_sb = [sb2(f"out_sb{i}", [128, D], F32) for i in range(2)]
            sc_ps = [st2.enter_context(nc.psum_tensor(f"sc_ps{i}", [128, 512], F32))
                     for i in range(2)]
            av_ps = [st2.enter_context(nc.psum_tensor(f"av_ps{i}", [128, 1024], F32))
                     for i in range(2)]
            r_ps = st2.enter_context(nc.psum_tensor("r_ps", [128, 8], F32))

            ktS = [sem["kt0"], sem["kt1"]]
            vvS = [sem["vv0"], sem["vv1"]]
            qtS = [sem["qt0"], sem["qt1"]]
            gS2 = [sem["gg0"], sem["gg1"]]
            xrS = [sem["gx0"], sem["gx1"]]
            obS = [sem["go0"], sem["go1"]]

            def sup_base(si):
                return 0 if si == 0 else SUPERS[si - 1]["step_end"]

            with nc.Block() as blk:
                @blk.sync
                def _(sync):
                    sync.wait_ge(sem["ph1"], 5)
                    kt_cnt = [0, 0]
                    v_cnt = [0, 0]
                    qt_cnt = [0, 0]
                    si = 0
                    for qg in range(QG):
                        qp = qg % 2
                        if qg >= 2:
                            prev_last = sum(range(1, qg)) - 1
                            sync.wait_ge(sem["sc_done"], SUPERS[prev_last]["step_end"])
                        if qg == 0:
                            qt_cnt[qp] += 1
                        else:
                            sync.dma_start(
                                out=qt_buf[qp][:].rearrange("p (c s) -> p c s", c=8),
                                in_=qT_h[:, :, qg * D:(qg + 1) * D].rearrange("c p s -> p c s")
                            ).then_inc(qtS[qp], 16)
                            qt_cnt[qp] += 1
                        for ks in range(qg + 1):
                            sp = si % 2
                            if si >= 2:
                                sync.wait_ge(sem["sc_done"], SUPERS[si - 2]["step_end"])
                            sync.dma_start(
                                out=kt_buf[sp][:].rearrange("p (c s) -> p c s", c=8),
                                in_=kT_h[:, :, ks * D:(ks + 1) * D].rearrange("c p s -> p c s")
                            ).then_inc(ktS[sp], 16)
                            kt_cnt[sp] += 1
                            if si >= 2:
                                sync.wait_ge(sem["av_done"], 8 * (si - 1))
                            if si == 0:
                                v_cnt[sp] += 1
                            else:
                                if si < 2:
                                    sync.wait_ge(sem["p2i"], 2)
                                sync.dma_start(
                                    out=v_buf[sp][:].rearrange("p (t z) -> p t z", t=8)[:, :, 0:D],
                                    in_=v_h[ks * D:(ks + 1) * D, :].rearrange("(t p) d -> p t d", p=128)
                                ).then_inc(vvS[sp], 16)
                                v_cnt[sp] += 1
                            si += 1

                @blk.tensor
                def _(tensor):
                    tensor.wait_ge(sem["ph1"], 5)
                    kt_cnt = [0, 0]
                    v_cnt = [0, 0]
                    qt_cnt = [0, 0]
                    gs = 0
                    gav = 0
                    for si, sup in enumerate(SUPERS):
                        qg, ks, diag = sup["qg"], sup["ks"], sup["diag"]
                        sp = si % 2
                        qp = qg % 2
                        kt_cnt[sp] += 1
                        tensor.wait_ge(ktS[sp], 16 * kt_cnt[sp])
                        if ks == 0:
                            qt_cnt[qp] += 1
                            tensor.wait_ge(qtS[qp], 16 * qt_cnt[qp])
                        for (kc, lo, hi, idx) in sup["steps"]:
                            if gs >= 2:
                                tensor.wait_ge(sem["exp_done"], gs - 1)
                            for c in range(8):
                                ins = tensor.matmul(
                                    sc_ps[gs % 2][:, 0:hi - lo],
                                    kt_buf[sp][:, c * D + kc * 128: c * D + kc * 128 + 128],
                                    qt_buf[qp][:, c * D + lo: c * D + hi],
                                    start=(c == 0), stop=(c == 7))
                            ins.then_inc(sem["sc_done"], 1)
                            gs += 1
                        if si == 0:
                            tensor.wait_ge(sem["vg0"], 16)
                        else:
                            v_cnt[sp] += 1
                            tensor.wait_ge(vvS[sp], 16 * v_cnt[sp])
                        tensor.wait_ge(sem["r_evac"], si)
                        waited_exp = 0
                        waited_mask = 0
                        for qt in range(8):
                            kcmax = qt if diag else 7
                            if gav >= 2:
                                tensor.wait_ge(sem["o_add"], gav - 1)
                            for kc in range(kcmax + 1):
                                need = sup["last"][kc] + 1
                                if need > waited_exp:
                                    tensor.wait_ge(sem["exp_done"], need)
                                    waited_exp = need
                                if diag and kc == qt:
                                    mneed = sup["masks"][kc] + 1
                                    if mneed > waited_mask:
                                        tensor.wait_ge(sem["vmask"], mneed)
                                        waited_mask = mneed
                                lhs = pt_buf[:, kc * D + qt * 128: kc * D + qt * 128 + 128]
                                for h in range(2):
                                    tensor.matmul(
                                        av_ps[qt % 2][:, h * 512:(h + 1) * 512], lhs,
                                        v_buf[sp][:, kc * 1032 + h * 512: kc * 1032 + (h + 1) * 512],
                                        start=(kc == 0), stop=(kc == kcmax))
                                ins = tensor.matmul(
                                    r_ps[:, qt:qt + 1], lhs,
                                    v_buf[sp][:, kc * 1032 + 1024: kc * 1032 + 1025],
                                    start=(kc == 0), stop=(kc == kcmax))
                            ins.then_inc(sem["av_done"], 1)
                            gav += 1

                @blk.scalar
                def _(scalar):
                    scalar.wait_ge(sem["ph1"], 5)
                    scalar.dma_start(
                        out=qt_buf[0][:].rearrange("p (c s) -> p c s", c=8),
                        in_=qT_h[:, :, 0:D].rearrange("c p s -> p c s")
                    ).then_inc(qtS[0], 16)
                    for si, sup in enumerate(SUPERS):
                        if si >= 1:
                            scalar.wait_ge(sem["av_done"], 8 * si)
                        for (kc, lo, hi, idx) in sup["steps"]:
                            scalar.wait_ge(sem["sc_done"], idx + 1)
                            scalar.activation(
                                pt_buf[:, kc * D + lo: kc * D + hi],
                                sc_ps[idx % 2][:, 0:hi - lo], AF.Exp,
                                bias=cpool[:, 2:3], scale=1.0 / 32.0
                            ).then_inc(sem["exp_done"], 1)

                @blk.vector
                def _(vector):
                    vector.wait_ge(sem["ph1"], 5)
                    for p in range(2):
                        vector.memset(v_buf[p][:], 1.0).then_inc(sem["p2i"], 1)
                    o_cnt = 0
                    r_cnt = 0
                    for si, sup in enumerate(SUPERS):
                        qg, ks, diag = sup["qg"], sup["ks"], sup["diag"]
                        qp = qg % 2
                        if diag:
                            for kc in range(8):
                                vector.wait_ge(sem["exp_done"], sup["first"][kc] + 1)
                                col = kc * D + kc * 128
                                vector.tensor_mul(pt_buf[:, col:col + 128],
                                                  pt_buf[:, col:col + 128],
                                                  dmask_sb[:])\
                                    .then_inc(sem["vmask"], 1)
                        for qt in range(8):
                            vector.wait_ge(sem["av_done"], 8 * si + qt + 1)
                            if o_cnt:
                                vector.wait_ge(sem["o_add"], o_cnt)
                            dst = o_acc[qp][:, qt * D:(qt + 1) * D]
                            if ks == 0:
                                if qg >= 2 and qt == 0:
                                    vector.wait_ge(sem["vf"], 24 * (qg - 1))
                                vector.tensor_copy(dst, av_ps[qt % 2][:])\
                                    .then_inc(sem["o_add"], 1)
                            else:
                                vector.tensor_add(dst, dst, av_ps[qt % 2][:])\
                                    .then_inc(sem["o_add"], 1)
                            o_cnt += 1
                        vector.wait_ge(sem["av_done"], 8 * (si + 1))
                        if r_cnt:
                            vector.wait_ge(sem["r_evac"], r_cnt)
                        if ks == 0:
                            if qg >= 2:
                                vector.wait_ge(sem["rv"], qg - 1)
                            vector.tensor_copy(r_acc[qp][:], r_ps[:, 0:8])\
                                .then_inc(sem["r_evac"], 1)
                        else:
                            vector.tensor_add(r_acc[qp][:], r_acc[qp][:], r_ps[:, 0:8])\
                                .then_inc(sem["r_evac"], 1)
                        r_cnt += 1
                        if diag:
                            vector.wait_ge(sem["r_evac"], r_cnt)
                            vector.reciprocal(rinv[qp][:], r_acc[qp][:])\
                                .then_inc(sem["rv"], 1)
                        if diag and qg == QG - 1:
                            n_adds = 8 * (qg * (qg + 1) // 2 + qg + 1)
                            vf2c = 0
                            guse = 0
                            for qt in range(1, 8, 2):
                                guse += 1
                                rows = (qg * 8 + qt) * 128
                                vector.wait_ge(sem["o_add"], n_adds)
                                vector.wait_ge(sem["rv"], qg + 1)
                                vector.wait_ge(gS2[1], 16 * (4 * qg + guse))
                                vector.wait_ge(xrS[1], 16 * (4 * qg + guse))
                                vector.wait_ge(obS[1], 16 * (qg * 4 + qt // 2))
                                if vf2c:
                                    vector.wait_ge(sem["vf2"], vf2c)
                                vector.tensor_mul(out_sb[1][:],
                                                  o_acc[qp][:, qt * D:(qt + 1) * D],
                                                  g_buf[1][:]).then_inc(sem["vf2"], 1)
                                vf2c += 1
                                vector.wait_ge(sem["vf2"], vf2c)
                                rb = rinv[qp][:, qt:qt + 1].broadcast_to([128, D])
                                vector.tensor_mul(out_sb[1][:], out_sb[1][:], rb)\
                                    .then_inc(sem["vf2"], 1)
                                vf2c += 1
                                vector.wait_ge(sem["vf2"], vf2c)
                                vector.tensor_add(out_sb[1][:], out_sb[1][:],
                                                  x2_buf[1][:]).then_inc(sem["vf2"], 1)
                                vf2c += 1

                @blk.gpsimd
                def _(gpsimd):
                    gpsimd.wait_ge(sem["ph1"], 5)
                    gpsimd.wait_ge(sem["p2i"], 2)
                    gpsimd.dma_start(
                        out=v_buf[0][:].rearrange("p (t z) -> p t z", t=8)[:, :, 0:D],
                        in_=v_h[0:D, :].rearrange("(t p) d -> p t d", p=128)
                    ).then_inc(sem["vg0"], 16)
                    g_cnt = [0, 0]
                    ob_cnt = [0, 0]
                    vf_cnt = 0
                    for qg in range(QG):
                        qp = qg % 2
                        vf_base = qg * 24
                        n_adds = 8 * (qg * (qg + 1) // 2 + qg + 1)
                        for qt in range(8):
                            rows = (qg * 8 + qt) * 128
                            par = qt % 2
                            if qt < 2:
                                if g_cnt[par] > 0:
                                    gpsimd.wait_ge(sem["vf"],
                                                   vf_base - 24 + 3 * (6 + qt) + 3)
                                gpsimd.dma_start(out=g_buf[par][:],
                                                 in_=g_h[rows:rows + 128, :])\
                                    .then_inc(gS2[par], 16)
                                gpsimd.dma_start(out=x2_buf[par][:],
                                                 in_=x_d[rows:rows + 128, :])\
                                    .then_inc(xrS[par], 16)
                                g_cnt[par] += 1
                            split = (qg == QG - 1 and par == 1)
                            if not split:
                                gpsimd.wait_ge(sem["o_add"], n_adds)
                                gpsimd.wait_ge(sem["rv"], qg + 1)
                                gpsimd.wait_ge(gS2[par], 16 * g_cnt[par])
                                gpsimd.wait_ge(xrS[par], 16 * g_cnt[par])
                                if qg > 0 or qt >= 2:
                                    gpsimd.wait_ge(obS[par], 16 * (qg * 4 + qt // 2))
                                if vf_cnt:
                                    gpsimd.wait_ge(sem["vf"], vf_cnt)
                                gpsimd.tensor_mul(res_sb[:],
                                                  o_acc[qp][:, qt * D:(qt + 1) * D],
                                                  g_buf[par][:]).then_inc(sem["vf"], 1)
                                vf_cnt += 1
                                gpsimd.wait_ge(sem["vf"], vf_cnt)
                                rb = rinv[qp][:, qt:qt + 1].broadcast_to([128, D])
                                gpsimd.tensor_mul(res_sb[:], res_sb[:], rb)\
                                    .then_inc(sem["vf"], 1)
                                vf_cnt += 1
                                gpsimd.wait_ge(sem["vf"], vf_cnt)
                                gpsimd.tensor_add(out_sb[par][:], res_sb[:],
                                                  x2_buf[par][:]).then_inc(sem["vf"], 1)
                                vf_cnt += 1
                                gpsimd.wait_ge(sem["vf"], vf_cnt)
                            else:
                                gpsimd.wait_ge(sem["vf2"], 3 * ((qt + 1) // 2))
                            gpsimd.dma_start(out=out_d[rows:rows + 128, :],
                                             in_=out_sb[par][:])\
                                .then_inc(obS[par], 16)
                            ob_cnt[par] += 1
                            if qt + 2 < 8:
                                rows2 = (qg * 8 + qt + 2) * 128
                                if split:
                                    gpsimd.wait_ge(sem["vf2"], 3 * ((qt + 1) // 2))
                                gpsimd.dma_start(out=g_buf[par][:],
                                                 in_=g_h[rows2:rows2 + 128, :])\
                                    .then_inc(gS2[par], 16)
                                gpsimd.dma_start(out=x2_buf[par][:],
                                                 in_=x_d[rows2:rows2 + 128, :])\
                                    .then_inc(xrS[par], 16)
                                g_cnt[par] += 1
                    gpsimd.wait_ge(obS[0], 16 * ob_cnt[0])
                    gpsimd.wait_ge(obS[1], 16 * ob_cnt[1])

    return nc


def _consts():
    i = np.arange(D, dtype=np.float64)
    ang = 2.0 * np.pi * np.outer(i, i) / D
    ang = np.mod(ang, 2.0 * np.pi)
    cos = np.cos(ang)
    sin = np.sin(ang)
    return dict(
        cosm=cos.astype(BF),
        sinm=sin.astype(BF),
        sinnm=(-sin).astype(BF),
        ccm=(cos / D).astype(BF),
        csnm=(-sin / D).astype(BF),
        ident=np.eye(128, dtype=BF),
        dmask=(np.tril(np.ones((128, 128), np.float32)).T).astype(BF),
    )


def kernel(L0_attn_x, L0_attn_W_q, L0_attn_W_k, L0_attn_W_v,
           L0_ffn_w_freq_real, L0_ffn_w_freq_imag, L0_ffn_bias):
    S = 8192
    nc = build_nc(S)
    consts = _consts()
    shared = dict(
        wq=np.ascontiguousarray(L0_attn_W_q, dtype=np.float32),
        wk=np.ascontiguousarray(L0_attn_W_k, dtype=np.float32),
        wv=np.ascontiguousarray(L0_attn_W_v, dtype=np.float32),
        wfr=np.ascontiguousarray(L0_ffn_w_freq_real, dtype=np.float32),
        wfi=np.ascontiguousarray(L0_ffn_w_freq_imag, dtype=np.float32),
        fbias=np.ascontiguousarray(L0_ffn_bias, dtype=np.float32).reshape(1, D),
        **consts,
    )
    x = np.asarray(L0_attn_x, dtype=np.float32)
    in_maps = []
    for b in range(8):
        m = dict(x=np.ascontiguousarray(x[b]), **shared)
        for nm in ("cosm", "sinm", "sinnm"):
            m[nm] = np.ascontiguousarray(shared[nm][:, b * 128:(b + 1) * 128])
        in_maps.append(m)
    res = run_bass_kernel_spmd(nc, in_maps, core_ids=list(range(8)))
    out = np.stack([np.asarray(res.results[b]["out"]) for b in range(8)], axis=0)
    return out.astype(np.float32)


# version 19
# speedup vs baseline: 1.1392x; 1.1267x over previous
import numpy as np
import ml_dtypes
from contextlib import ExitStack

import concourse.bass as bass
import concourse.mybir as mybir
from concourse.bass_utils import run_bass_kernel_spmd

F32 = mybir.dt.float32
BF16 = mybir.dt.bfloat16
AF = mybir.ActivationFunctionType
ALU = mybir.AluOpType
AX = mybir.AxisListType
BF = ml_dtypes.bfloat16

D = 1024
NCHUNK = 8
EXP_SHIFT = -12.0


def p2_supers(S):
    QG = S // 1024
    supers = []
    step_idx = 0
    mask_idx = 0
    for qg in range(QG):
        for ks in range(qg + 1):
            diag = ks == qg
            steps = []
            first_step = {}
            last_step = {}
            for kc in range(8):
                lo = kc * 128 if diag else 0
                subs = []
                if lo < 512:
                    subs = [(lo, 512), (512, 1024)]
                else:
                    subs = [(lo, 1024)]
                for (a, b) in subs:
                    if kc not in first_step:
                        first_step[kc] = step_idx
                    last_step[kc] = step_idx
                    steps.append((kc, a, b, step_idx))
                    step_idx += 1
            masks = {}
            if diag:
                for kc in range(8):
                    masks[kc] = mask_idx
                    mask_idx += 1
            supers.append(dict(qg=qg, ks=ks, diag=diag, steps=steps,
                               first=first_step, last=last_step, masks=masks,
                               step_end=step_idx))
    return supers


def build_nc(S=8192, n_cores=8):
    G = S // 512
    QG = S // 1024
    SUPERS = p2_supers(S)
    NSUP = len(SUPERS)

    nc = bass.Bass("TRN2", target_bir_lowering=False, debug=False)

    x_d = nc.dram_tensor("x", [S, D], F32, kind="ExternalInput").ap()
    wq_d = nc.dram_tensor("wq", [D, D], F32, kind="ExternalInput").ap()
    wk_d = nc.dram_tensor("wk", [D, D], F32, kind="ExternalInput").ap()
    wv_d = nc.dram_tensor("wv", [D, D], F32, kind="ExternalInput").ap()
    wfr_d = nc.dram_tensor("wfr", [D, D], F32, kind="ExternalInput").ap()
    wfi_d = nc.dram_tensor("wfi", [D, D], F32, kind="ExternalInput").ap()
    fb_d = nc.dram_tensor("fbias", [1, D], F32, kind="ExternalInput").ap()
    cos_d = nc.dram_tensor("cosm", [D, D], BF16, kind="ExternalInput").ap()
    sin_d = nc.dram_tensor("sinm", [D, D], BF16, kind="ExternalInput").ap()
    sinn_d = nc.dram_tensor("sinnm", [D, D], BF16, kind="ExternalInput").ap()
    cc_d = nc.dram_tensor("ccm", [D, D], BF16, kind="ExternalInput").ap()
    csn_d = nc.dram_tensor("csnm", [D, D], BF16, kind="ExternalInput").ap()
    id_d = nc.dram_tensor("ident", [128, 128], BF16, kind="ExternalInput").ap()
    dm_d = nc.dram_tensor("dmask", [128, 128], BF16, kind="ExternalInput").ap()
    out_d = nc.dram_tensor("out", [S, D], F32, kind="ExternalOutput").ap()

    qT_h = nc.dram_tensor("qT_h", [8, 128, S], BF16, kind="Internal").ap()
    kT_h = nc.dram_tensor("kT_h", [8, 128, S], BF16, kind="Internal").ap()
    v_h = nc.dram_tensor("v_h", [S, D], BF16, kind="Internal").ap()
    g_h = nc.dram_tensor("g_h", [S, D], F32, kind="Internal").ap()
    xn_h = nc.dram_tensor("xn_h", [S, D], BF16, kind="Internal").ap()

    with ExitStack() as outer:
        sem = {}
        for name in (["dGa", "dS1", "ph0", "ph1",
                     "t_p0", "sc_p0",
                     "x0", "x1", "qS0", "kS0", "vS0", "gS",
                     "vln", "s_ln", "t_qkvl", "v_ev", "v_ev2", "sc_sil", "rv",
                     "xnst0", "xnst1", "xnl0", "xnl1",
                     "sc_done", "exp_done", "vmask", "av_done", "o_add",
                     "r_evac", "vf", "p2i",
                     "kt0", "kt1", "vv0", "vv1", "qt0", "qt1",
                     "gg0", "gg1", "gx0", "gx1", "go0", "go1"]
                     + [f"dGq{i}" for i in range(8)]
                     + [f"dGk{i}" for i in range(8)]
                     + [f"dGv{i}" for i in range(8)]
                     + ["dStrg", "vg0", "vf2"]
                     + [f"dGr{i}" for i in range(8)] + [f"dGi{i}" for i in range(8)]):
            sem[name] = outer.enter_context(nc.semaphore(name))

        def osb(name, shape, dt):
            return outer.enter_context(nc.sbuf_tensor(name, shape, dt))
        cpool = osb("cpool", [128, 4], F32)
        ones_row = osb("ones_row", [1, 128], BF16)
        ident_sb = osb("ident_sb", [128, 128], BF16)
        dmask_sb = osb("dmask_sb", [128, 128], BF16)
        mil_sb = osb("mil_sb", [128, 4], mybir.dt.int32)

        st01 = ExitStack()

        def osb01(name, shape, dt):
            return st01.enter_context(nc.sbuf_tensor(name, shape, dt))
        m_sb = osb01("m_sb", [128, 8 * D], BF16)
        bias_sb = osb01("bias_sb", [1, D], BF16)
        x_buf0 = osb01("x_buf0", [128, 4 * D], F32)
        stats = osb01("stats", [128, 32], F32)
        xc4 = [osb01(f"xc4_{i}", [128, D], F32) for i in range(4)]
        xsq_sb = osb01("xsq_sb", [128, D], F32)
        xn4 = [osb01(f"xn4_{i}", [128, D], BF16) for i in range(4)]
        xnT = [osb01(f"xnT{i}", [128, 8 * 512], BF16) for i in range(2)]
        wq_sb = osb01("wq_sb", [128, 8 * D], BF16)
        wk_sb = osb01("wk_sb", [128, 8 * D], BF16)
        wv_sb = osb01("wv_sb", [128, 8 * D], BF16)

        with ExitStack() as st0:
            def sb0(name, shape, dt):
                return st0.enter_context(nc.sbuf_tensor(name, shape, dt))
            wr_sb = sb0("wr_sb", [128, 8 * D], BF16)
            wi_sb = sb0("wi_sb", [128, 8 * D], BF16)
            trgC = [sb0(f"trgC{i}", [128, 8 * 128], BF16) for i in range(2)]
            trgS = [sb0(f"trgS{i}", [128, 8 * 128], BF16) for i in range(2)]
            trgSN = [sb0(f"trgSN{i}", [128, 8 * 128], BF16) for i in range(2)]
            trgCC = sb0("trgCC", [128, 8 * D], BF16)
            trgCSN = sb0("trgCSN", [128, 8 * D], BF16)
            art_sb = sb0("art_sb", [128, 8 * 128], BF16)
            ait_sb = sb0("ait_sb", [128, 8 * 128], BF16)
            pp = st0.enter_context(nc.psum_tensor("pp", [128, 2048], F32))

            def p0_groups(b):
                bp = b % 2
                gs = []
                for j in range(8):
                    gs.append(("art", j, (wr_sb, trgC[bp]), (wi_sb, trgS[bp])))
                for j in range(8):
                    gs.append(("ait", j, (wi_sb, trgC[bp]), (wr_sb, trgSN[bp])))
                for ub in range(2):
                    gs.append(("m", ub, (art_sb, trgCC), (ait_sb, trgCSN)))
                return gs

            def trg_load(sync, b):
                for srcd, dst in [(cos_d, trgC), (sin_d, trgS), (sinn_d, trgSN)]:
                    sync.dma_start(
                        out=dst[b % 2][:].rearrange("p (c e) -> p c e", c=8),
                        in_=srcd[:, b * 128:(b + 1) * 128]
                        .rearrange("(c p) e -> p c e", p=128)
                    ).then_inc(sem["dStrg"], 16)

            with nc.Block() as blk:
                @blk.sync
                def _(sync):
                    trg_load(sync, 0)
                    trg_load(sync, 1)
                    sync.dma_start(
                        out=x_buf0[:].rearrange("p (t d) -> p t d", t=4),
                        in_=x_d[0:512, :].rearrange("(t p) d -> p t d", p=128)
                    ).then_inc(sem["x0"], 16)
                    sync.dma_start(out=ident_sb[:], in_=id_d).then_inc(sem["dS1"], 16)
                    sync.dma_start(out=dmask_sb[:], in_=dm_d).then_inc(sem["dS1"], 16)
                    for srcd, dst in [(cc_d, trgCC), (csn_d, trgCSN)]:
                        sync.dma_start(
                            out=dst[:].rearrange("p (c e) -> p c e", c=8),
                            in_=srcd.rearrange("(c p) e -> p c e", p=128)
                        ).then_inc(sem["dS1"], 16)
                    for b in range(2, 8):
                        sync.wait_ge(sem["t_p0"], 18 * (b - 2) + 16)
                        trg_load(sync, b)
                        if b == 4:
                            for t in range(4):
                                sync.wait_ge(sem["vln"], 21 + t)
                                sync.dma_start(out=xn_h[t * 128:(t + 1) * 128, :],
                                               in_=xn4[t][:]).then_inc(sem["xnst0"], 16)
                        if b == 5:
                            sync.wait_ge(sem["xnst0"], 64)
                            for c in range(4):
                                sync.dma_start_transpose(
                                    xnT[0][:, c * 512:(c + 1) * 512],
                                    xn_h[0:512, c * 128:(c + 1) * 128]
                                ).then_inc(sem["xnl0"], 16)
                        if b == 6:
                            for c in range(4, 8):
                                sync.dma_start_transpose(
                                    xnT[0][:, c * 512:(c + 1) * 512],
                                    xn_h[0:512, c * 128:(c + 1) * 128]
                                ).then_inc(sem["xnl0"], 16)
                    sync.wait_ge(sem["dStrg"], 48 * 8)
                    sync.wait_ge(sem["dS1"], 64)
                    sync.nop().then_inc(sem["ph0"], 1)

                @blk.gpsimd
                def _(gpsimd):
                    for srcd, dst, pref in [(wfr_d, wr_sb, "dGr"), (wfi_d, wi_sb, "dGi")]:
                        for j in range(8):
                            gpsimd.dma_start(
                                out=dst[:].rearrange("p (c e) -> p c e", c=8)
                                [:, :, j * 128:(j + 1) * 128],
                                in_=srcd.rearrange("(c p) e -> p c e", p=128)
                                [:, :, j * 128:(j + 1) * 128]
                            ).then_inc(sem[f"{pref}{j}"], 16)
                    for srcd, dst, pref in [(wq_d, wq_sb, "dGq"), (wk_d, wk_sb, "dGk")]:
                        for e in range(8):
                            gpsimd.dma_start(
                                out=dst[:].rearrange("p (c e) -> p c e", c=8)
                                [:, :, e * 128:(e + 1) * 128],
                                in_=srcd.rearrange("(c p) e -> p c e", p=128)
                                [:, :, e * 128:(e + 1) * 128]
                            ).then_inc(sem[f"{pref}{e}"], 16)
                    gpsimd.dma_start(out=bias_sb[:], in_=fb_d).then_inc(sem["dGa"], 16)
                    for e in range(8):
                        gpsimd.dma_start(
                            out=wv_sb[:].rearrange("p (c e) -> p c e", c=8)
                            [:, :, e * 128:(e + 1) * 128],
                            in_=wv_d.rearrange("(c p) e -> p c e", p=128)
                            [:, :, e * 128:(e + 1) * 128]
                        ).then_inc(sem[f"dGv{e}"], 16)
                    gpsimd.nop().then_inc(sem["ph0"], 1)

                @blk.vector
                def _(vector):
                    vector.memset(ones_row[:], 1.0)
                    vector.memset(cpool[:, 0:1], 1e-5)
                    vector.memset(cpool[:, 1:2], 0.0)
                    vector.memset(cpool[:, 2:3], EXP_SHIFT)
                    vcnt = 0
                    vector.wait_ge(sem["x0"], 16)
                    for t in range(4):
                        xt = x_buf0[:, t * D:(t + 1) * D]
                        c0 = 8 * t
                        if vcnt:
                            vector.wait_ge(sem["vln"], vcnt)
                        vector.reduce_sum(stats[:, c0:c0 + 1], xt, axis=AX.X)\
                            .then_inc(sem["vln"], 1); vcnt += 1
                        vector.wait_ge(sem["vln"], vcnt)
                        vector.tensor_scalar_mul(stats[:, c0 + 1:c0 + 2],
                                                 stats[:, c0:c0 + 1], -1.0 / D)\
                            .then_inc(sem["vln"], 1); vcnt += 1
                        vector.wait_ge(sem["vln"], vcnt)
                        vector.tensor_scalar_add(xc4[t][:], xt,
                                                 stats[:, c0 + 1:c0 + 2])\
                            .then_inc(sem["vln"], 1); vcnt += 1
                        vector.wait_ge(sem["vln"], vcnt)
                        vector.tensor_mul(xsq_sb[:], xc4[t][:], xc4[t][:])\
                            .then_inc(sem["vln"], 1); vcnt += 1
                        vector.wait_ge(sem["vln"], vcnt)
                        vector.reduce_sum(stats[:, c0 + 2:c0 + 3], xsq_sb[:],
                                          axis=AX.X)\
                            .then_inc(sem["vln"], 1); vcnt += 1
                    for t in range(4):
                        vector.wait_ge(sem["s_ln"], 2 * t + 2)
                        vector.wait_ge(sem["vln"], vcnt)
                        vector.tensor_scalar_mul(xn4[t][:], xc4[t][:],
                                                 stats[:, 8 * t + 4:8 * t + 5])\
                            .then_inc(sem["vln"], 1); vcnt += 1
                    vector.memset(mil_sb[:, 0:1], 0).then_inc(sem["ph0"], 1)

                @blk.tensor
                def _(tensor):
                    for b in range(8):
                        for g18, (kind, j, (l1, r1), (l2, r2)) in enumerate(p0_groups(b)):
                            gi = 18 * b + g18
                            if g18 == 0:
                                tensor.wait_ge(sem["dStrg"], 48 * (b + 1))
                            if b == 0 and kind == "art":
                                tensor.wait_ge(sem[f"dGr{j}"], 16)
                            if b == 0 and kind == "ait":
                                tensor.wait_ge(sem[f"dGi{j}"], 16)
                                tensor.wait_ge(sem[f"dGr{j}"], 16)
                            if g18 == 16:
                                tensor.wait_ge(sem["sc_p0"], 18 * b + 16)
                                if b == 0:
                                    tensor.wait_ge(sem["dS1"], 64)
                            if gi >= 4:
                                tensor.wait_ge(sem["sc_p0"], gi - 3)
                            ncols = 128 if kind != "m" else 512
                            ps = pp[:, (gi % 4) * 512:(gi % 4) * 512 + ncols]
                            for c in range(8):
                                if kind == "m":
                                    lhs = l1[:, c * 128: c * 128 + 128]
                                    rhs = r1[:, c * D + j * 512: c * D + j * 512 + 512]
                                else:
                                    lhs = l1[:, c * D + j * 128: c * D + j * 128 + 128]
                                    rhs = r1[:, c * 128:(c + 1) * 128]
                                tensor.matmul(ps, lhs, rhs, start=(c == 0), stop=False)
                            if b == 0 and kind == "art":
                                tensor.wait_ge(sem[f"dGi{j}"], 16)
                            for c in range(8):
                                if kind == "m":
                                    lhs = l2[:, c * 128: c * 128 + 128]
                                    rhs = r2[:, c * D + j * 512: c * D + j * 512 + 512]
                                else:
                                    lhs = l2[:, c * D + j * 128: c * D + j * 128 + 128]
                                    rhs = r2[:, c * 128:(c + 1) * 128]
                                ins = tensor.matmul(ps, lhs, rhs, start=False, stop=(c == 7))
                            ins.then_inc(sem["t_p0"], 1)
                    tensor.nop().then_inc(sem["ph0"], 1)

                @blk.scalar
                def _(scalar):
                    def p0_copies(b):
                        for g18, (kind, j, _, _2) in enumerate(p0_groups(b)):
                            gi = 18 * b + g18
                            scalar.wait_ge(sem["t_p0"], gi + 1)
                            if kind == "m":
                                dst = m_sb[:, b * D + j * 512: b * D + j * 512 + 512]
                                ps = pp[:, (gi % 4) * 512:(gi % 4) * 512 + 512]
                            else:
                                dst = (art_sb if kind == "art" else ait_sb)[:, j * 128: j * 128 + 128]
                                ps = pp[:, (gi % 4) * 512:(gi % 4) * 512 + 128]
                            scalar.copy(dst, ps).then_inc(sem["sc_p0"], 1)
                    for b in range(2):
                        p0_copies(b)
                    for t in range(4):
                        c0 = 8 * t
                        scalar.wait_ge(sem["vln"], 5 * t + 5)
                        if t >= 1:
                            scalar.wait_ge(sem["s_ln"], 2 * t)
                        scalar.activation(stats[:, c0 + 3:c0 + 4],
                                          stats[:, c0 + 2:c0 + 3], AF.Ln,
                                          bias=cpool[:, 0:1], scale=1.0 / D)\
                            .then_inc(sem["s_ln"], 1)
                        scalar.wait_ge(sem["s_ln"], 2 * t + 1)
                        scalar.activation(stats[:, c0 + 4:c0 + 5],
                                          stats[:, c0 + 3:c0 + 4], AF.Exp,
                                          bias=cpool[:, 1:2], scale=-0.5)\
                            .then_inc(sem["s_ln"], 1)
                    for b in range(2, 8):
                        p0_copies(b)
                    scalar.memzero(mil_sb[:, 1:2]).then_inc(sem["ph0"], 1)

        with ExitStack() as st1:
            def sb1(name, shape, dt):
                return st1.enter_context(nc.sbuf_tensor(name, shape, dt))
            x_buf = [x_buf0, sb1("x_buf1", [128, 4 * D], F32)]
            qstage = sb1("qstage", [128, 8 * 512], BF16)
            kstage = sb1("kstage", [128, 8 * 512], BF16)
            vstage = sb1("vstage", [128, 4 * D], BF16)
            gstage = sb1("gstage", [128, 4 * D], F32)
            lin_stage = sb1("lin_stage", [128, 4 * D], F32)
            qk_ps = [st1.enter_context(nc.psum_tensor(f"qk_ps{i}", [128, 512], F32))
                     for i in range(4)]
            v_ps = st1.enter_context(nc.psum_tensor("v_ps", [128, 1024], F32))
            l_ps = st1.enter_context(nc.psum_tensor("l_ps", [128, 1024], F32))

            xsem = [sem["x0"], sem["x1"]]
            qS = sem["qS0"]
            kS = sem["kS0"]
            vS = sem["vS0"]

            def vln_val(g, t, k):
                return 24 * g + 6 * t + k

            with nc.Block() as blk:
                @blk.sync
                def _(sync):
                    sync.wait_ge(sem["ph0"], 5)
                    xnstS = [sem["xnst0"], sem["xnst1"]]
                    xnlS = [sem["xnl0"], sem["xnl1"]]
                    for g in range(G):
                        if g >= 1:
                            for t in range(4):
                                ti = 4 * g + t
                                sync.wait_ge(sem["vln"], 24 * g + 21 + t)
                                sync.dma_start(out=xn_h[ti * 128:(ti + 1) * 128, :],
                                               in_=xn4[ti % 4][:])\
                                    .then_inc(xnstS[g % 2], 16)
                        if g >= 1:
                            sync.wait_ge(xnstS[g % 2], 64 * (g // 2 + 1))
                            if g >= 2:
                                sync.wait_ge(sem["t_qkvl"], 32 * (g - 1))
                            for c in range(8):
                                sync.dma_start_transpose(
                                    xnT[g % 2][:, c * 512:(c + 1) * 512],
                                    xn_h[g * 512:(g + 1) * 512, c * 128:(c + 1) * 128]
                                ).then_inc(xnlS[g % 2], 16)
                        if g == 0 and G > 1:
                            sync.dma_start(
                                out=x_buf[1][:].rearrange("p (t d) -> p t d", t=4),
                                in_=x_d[512:1024, :].rearrange("(t p) d -> p t d", p=128)
                            ).then_inc(xsem[1], 16)
                        if g + 2 < G:
                            sync.wait_ge(sem["vln"], 24 * (g + 1))
                            sync.dma_start(
                                out=x_buf[g % 2][:].rearrange("p (t d) -> p t d", t=4),
                                in_=x_d[(g + 2) * 512:(g + 3) * 512, :]
                                .rearrange("(t p) d -> p t d", p=128)
                            ).then_inc(xsem[g % 2], 16)
                    sync.wait_ge(qS, 16 * G)
                    sync.wait_ge(kS, 16 * G)
                    sync.wait_ge(vS, 16 * G)
                    sync.wait_ge(sem["gS"], 16 * G)
                    sync.nop().then_inc(sem["ph1"], 1)

                @blk.vector
                def _(vector):
                    vector.wait_ge(sem["ph0"], 5)
                    vcnt = 24
                    for g in range(G + 1):
                        if 1 <= g < G:
                            vector.wait_ge(xsem[g % 2], 16 * (g // 2 + 1))
                            for t in range(4):
                                xt = x_buf[g % 2][:, t * D:(t + 1) * D]
                                ti = 4 * g + t
                                c0 = 8 * t
                                if g >= 1:
                                    vector.wait_ge(sem["s_ln"], 2 * (4 * (g - 1) + t) + 2)
                                if vcnt:
                                    vector.wait_ge(sem["vln"], vcnt)
                                vector.reduce_sum(stats[:, c0:c0 + 1], xt, axis=AX.X)\
                                    .then_inc(sem["vln"], 1); vcnt += 1
                                vector.wait_ge(sem["vln"], vcnt)
                                vector.tensor_scalar_mul(stats[:, c0 + 1:c0 + 2],
                                                         stats[:, c0:c0 + 1], -1.0 / D)\
                                    .then_inc(sem["vln"], 1); vcnt += 1
                                vector.wait_ge(sem["vln"], vcnt)
                                vector.tensor_scalar_add(xc4[t][:], xt,
                                                         stats[:, c0 + 1:c0 + 2])\
                                    .then_inc(sem["vln"], 1); vcnt += 1
                                vector.wait_ge(sem["vln"], vcnt)
                                vector.tensor_mul(xsq_sb[:], xc4[t][:], xc4[t][:])\
                                    .then_inc(sem["vln"], 1); vcnt += 1
                                vector.wait_ge(sem["vln"], vcnt)
                                vector.reduce_sum(stats[:, c0 + 2:c0 + 3], xsq_sb[:],
                                                  axis=AX.X)\
                                    .then_inc(sem["vln"], 1); vcnt += 1
                            for t in range(4):
                                ti = 4 * g + t
                                vector.wait_ge(sem["s_ln"], 2 * ti + 2)
                                if g >= 1:
                                    xnstS = [sem["xnst0"], sem["xnst1"]]
                                    vector.wait_ge(xnstS[(g - 1) % 2],
                                                   64 * ((g - 1) // 2 + 1))
                                vector.wait_ge(sem["vln"], vcnt)
                                vector.tensor_scalar_mul(xn4[ti % 4][:], xc4[t][:],
                                                         stats[:, 8 * t + 4:8 * t + 5])\
                                    .then_inc(sem["vln"], 1); vcnt += 1
                        if g >= 1:
                            gg = g - 1
                            for t in range(4):
                                if t == 0 and gg >= 1:
                                    vector.wait_ge(vS, 16 * gg)
                                for k, (src_ps, dst, h) in enumerate(
                                        [(v_ps, vstage, 0), (l_ps, lin_stage, 0),
                                         (v_ps, vstage, 1), (l_ps, lin_stage, 1)]):
                                    vector.wait_ge(sem["t_qkvl"],
                                                   32 * gg + 16 + 4 * t + k + 1)
                                    if k == 1 and gg >= 1:
                                        vector.wait_ge(sem["sc_sil"],
                                                       4 * (gg - 1) + t + 1)
                                    vector.tensor_copy(
                                        dst[:, t * D + h * 512: t * D + h * 512 + 512],
                                        src_ps[:, h * 512:(h + 1) * 512])\
                                        .then_inc(sem["v_ev2"], 1)
                    vector.memset(mil_sb[:, 0:1], 0).then_inc(sem["ph1"], 1)

                @blk.scalar
                def _(scalar):
                    scalar.wait_ge(sem["ph0"], 5)
                    for g in range(G + 1):
                        if g >= 1:
                            gg = g - 1
                            for j in range(16):
                                scalar.wait_ge(sem["t_qkvl"], 32 * gg + j + 1)
                                if j == 0 and gg >= 1:
                                    scalar.wait_ge(qS, 16 * gg)
                                if j == 8 and gg >= 1:
                                    scalar.wait_ge(kS, 16 * gg)
                                stg = qstage if j < 8 else kstage
                                e = j % 8
                                scalar.copy(stg[:, e * 512:(e + 1) * 512],
                                            qk_ps[j % 4][:])\
                                    .then_inc(sem["v_ev"], 1)
                            scalar.wait_ge(sem["v_ev"], 16 * gg + 16)
                            scalar.dma_start(
                                out=qT_h[:, :, gg * 512:(gg + 1) * 512]
                                .rearrange("c p s -> p c s"),
                                in_=qstage[:].rearrange("p (c s) -> p c s", c=8)
                            ).then_inc(qS, 16)
                            scalar.dma_start(
                                out=kT_h[:, :, gg * 512:(gg + 1) * 512]
                                .rearrange("c p s -> p c s"),
                                in_=kstage[:].rearrange("p (c s) -> p c s", c=8)
                            ).then_inc(kS, 16)
                        if 1 <= g < G:
                            for t in range(4):
                                ti = 4 * g + t
                                c0 = 8 * t
                                scalar.wait_ge(sem["vln"], 24 * g + 5 * t + 5)
                                if ti >= 1:
                                    scalar.wait_ge(sem["s_ln"], 2 * ti)
                                scalar.activation(stats[:, c0 + 3:c0 + 4],
                                                  stats[:, c0 + 2:c0 + 3], AF.Ln,
                                                  bias=cpool[:, 0:1], scale=1.0 / D)\
                                    .then_inc(sem["s_ln"], 1)
                                scalar.wait_ge(sem["s_ln"], 2 * ti + 1)
                                scalar.activation(stats[:, c0 + 4:c0 + 5],
                                                  stats[:, c0 + 3:c0 + 4], AF.Exp,
                                                  bias=cpool[:, 1:2], scale=-0.5)\
                                    .then_inc(sem["s_ln"], 1)
                        if g >= 1:
                            gg = g - 1
                            for t in range(4):
                                scalar.wait_ge(sem["v_ev2"], 16 * gg + 4 * t + 4)
                                if t == 0 and gg >= 1:
                                    scalar.wait_ge(sem["gS"], 16 * gg)
                                scalar.activation(gstage[:, t * D:(t + 1) * D],
                                                  lin_stage[:, t * D:(t + 1) * D],
                                                  AF.Silu).then_inc(sem["sc_sil"], 1)
                            scalar.wait_ge(sem["v_ev2"], 16 * gg + 16)
                            scalar.wait_ge(sem["sc_sil"], 4 * gg + 4)
                            scalar.dma_start(
                                out=v_h[gg * 512:(gg + 1) * 512, :]
                                .rearrange("(t p) d -> p t d", p=128),
                                in_=vstage[:].rearrange("p (t d) -> p t d", t=4)
                            ).then_inc(vS, 16)
                            scalar.dma_start(
                                out=g_h[gg * 512:(gg + 1) * 512, :]
                                .rearrange("(t p) d -> p t d", p=128),
                                in_=gstage[:].rearrange("p (t d) -> p t d", t=4)
                            ).then_inc(sem["gS"], 16)
                    scalar.memzero(mil_sb[:, 1:2]).then_inc(sem["ph1"], 1)

                @blk.tensor
                def _(tensor):
                    tensor.wait_ge(sem["ph0"], 5)
                    tensor.wait_ge(sem["dGa"], 16)
                    xnlS = [sem["xnl0"], sem["xnl1"]]
                    for g in range(G):
                        tensor.wait_ge(xnlS[g % 2], 128 * (g // 2 + 1))
                        for j in range(16):
                            if g == 0:
                                if j < 8:
                                    tensor.wait_ge(sem[f"dGq{j}"], 16)
                                else:
                                    tensor.wait_ge(sem[f"dGk{j % 8}"], 16)
                            w = wq_sb if j < 8 else wk_sb
                            e = j % 8
                            qj = 16 * g + j
                            if qj >= 4:
                                tensor.wait_ge(sem["v_ev"], qj - 3)
                            for c in range(8):
                                ins = tensor.matmul(
                                    qk_ps[qj % 4][:],
                                    w[:, c * D + e * 128: c * D + e * 128 + 128],
                                    xnT[g % 2][:, c * 512:(c + 1) * 512],
                                    start=(c == 0), stop=(c == 7))
                            ins.then_inc(sem["t_qkvl"], 1)
                        if g == 0:
                            for e in range(8):
                                tensor.wait_ge(sem[f"dGv{e}"], 16)
                        for t in range(4):
                            lhs = [xnT[g % 2][:, c * 512 + t * 128: c * 512 + t * 128 + 128]
                                   for c in range(8)]
                            base = 16 * g + 4 * (t - 1)
                            for k, (dst_ps, w, h, is_lin) in enumerate(
                                    [(v_ps, wv_sb, 0, False), (l_ps, m_sb, 0, True),
                                     (v_ps, wv_sb, 1, False), (l_ps, m_sb, 1, True)]):
                                if t > 0 or g > 0:
                                    tensor.wait_ge(sem["v_ev2"], max(0, base + k + 1))
                                for c in range(8):
                                    ins = tensor.matmul(
                                        dst_ps[:, h * 512:(h + 1) * 512], lhs[c],
                                        w[:, c * D + h * 512: c * D + (h + 1) * 512],
                                        start=(c == 0),
                                        stop=(c == 7 and not is_lin))
                                if is_lin:
                                    ins = tensor.matmul(
                                        dst_ps[:, h * 512:(h + 1) * 512],
                                        ones_row[0:1, :],
                                        bias_sb[0:1, h * 512:(h + 1) * 512],
                                        start=False, stop=True)
                                ins.then_inc(sem["t_qkvl"], 1)
                    tensor.nop().then_inc(sem["ph1"], 1)

                @blk.gpsimd
                def _(gpsimd):
                    gpsimd.wait_ge(sem["ph0"], 5)
                    gpsimd.nop().then_inc(sem["ph1"], 1)

        st01.close()
        with ExitStack() as st2:
            def sb2(name, shape, dt):
                return st2.enter_context(nc.sbuf_tensor(name, shape, dt))
            qt_buf = [sb2(f"qt_buf{i}", [128, 8 * D], BF16) for i in range(2)]
            kt_buf = [sb2(f"kt_buf{i}", [128, 8 * D], BF16) for i in range(2)]
            v_buf = [sb2(f"v_buf{i}", [128, 8 * 1032], BF16) for i in range(2)]
            pt_buf = sb2("pt_buf", [128, 8 * D], BF16)
            o_acc = [sb2(f"o_acc{i}", [128, 8 * D], F32) for i in range(2)]
            r_acc = [sb2(f"r_acc{i}", [128, 8], F32) for i in range(2)]
            rinv = [sb2(f"rinv{i}", [128, 8], F32) for i in range(2)]
            g_buf = [sb2(f"g_buf{i}", [128, D], F32) for i in range(2)]
            x2_buf = [sb2(f"x2_buf{i}", [128, D], F32) for i in range(2)]
            res_sb = sb2("res_sb", [128, D], F32)
            out_sb = [sb2(f"out_sb{i}", [128, D], F32) for i in range(2)]
            sc_ps = [st2.enter_context(nc.psum_tensor(f"sc_ps{i}", [128, 512], F32))
                     for i in range(2)]
            av_ps = [st2.enter_context(nc.psum_tensor(f"av_ps{i}", [128, 1024], F32))
                     for i in range(2)]
            r_ps = st2.enter_context(nc.psum_tensor("r_ps", [128, 8], F32))

            ktS = [sem["kt0"], sem["kt1"]]
            vvS = [sem["vv0"], sem["vv1"]]
            qtS = [sem["qt0"], sem["qt1"]]
            gS2 = [sem["gg0"], sem["gg1"]]
            xrS = [sem["gx0"], sem["gx1"]]
            obS = [sem["go0"], sem["go1"]]

            def sup_base(si):
                return 0 if si == 0 else SUPERS[si - 1]["step_end"]

            with nc.Block() as blk:
                @blk.sync
                def _(sync):
                    sync.wait_ge(sem["ph1"], 5)
                    kt_cnt = [0, 0]
                    v_cnt = [0, 0]
                    qt_cnt = [0, 0]
                    si = 0
                    for qg in range(QG):
                        qp = qg % 2
                        if qg >= 2:
                            prev_last = sum(range(1, qg)) - 1
                            sync.wait_ge(sem["sc_done"], SUPERS[prev_last]["step_end"])
                        if qg == 0:
                            qt_cnt[qp] += 1
                        else:
                            sync.dma_start(
                                out=qt_buf[qp][:].rearrange("p (c s) -> p c s", c=8),
                                in_=qT_h[:, :, qg * D:(qg + 1) * D].rearrange("c p s -> p c s")
                            ).then_inc(qtS[qp], 16)
                            qt_cnt[qp] += 1
                        for ks in range(qg + 1):
                            sp = si % 2
                            if si >= 2:
                                sync.wait_ge(sem["sc_done"], SUPERS[si - 2]["step_end"])
                            sync.dma_start(
                                out=kt_buf[sp][:].rearrange("p (c s) -> p c s", c=8),
                                in_=kT_h[:, :, ks * D:(ks + 1) * D].rearrange("c p s -> p c s")
                            ).then_inc(ktS[sp], 16)
                            kt_cnt[sp] += 1
                            if si >= 2:
                                sync.wait_ge(sem["av_done"], 8 * (si - 1))
                            if si == 0:
                                v_cnt[sp] += 1
                            else:
                                if si < 2:
                                    sync.wait_ge(sem["p2i"], 2)
                                sync.dma_start(
                                    out=v_buf[sp][:].rearrange("p (t z) -> p t z", t=8)[:, :, 0:D],
                                    in_=v_h[ks * D:(ks + 1) * D, :].rearrange("(t p) d -> p t d", p=128)
                                ).then_inc(vvS[sp], 16)
                                v_cnt[sp] += 1
                            si += 1

                @blk.tensor
                def _(tensor):
                    tensor.wait_ge(sem["ph1"], 5)
                    kt_cnt = [0, 0]
                    v_cnt = [0, 0]
                    qt_cnt = [0, 0]
                    gs = 0
                    gav = 0
                    for si, sup in enumerate(SUPERS):
                        qg, ks, diag = sup["qg"], sup["ks"], sup["diag"]
                        sp = si % 2
                        qp = qg % 2
                        kt_cnt[sp] += 1
                        tensor.wait_ge(ktS[sp], 16 * kt_cnt[sp])
                        if ks == 0:
                            qt_cnt[qp] += 1
                            tensor.wait_ge(qtS[qp], 16 * qt_cnt[qp])
                        for (kc, lo, hi, idx) in sup["steps"]:
                            if gs >= 2:
                                tensor.wait_ge(sem["exp_done"], gs - 1)
                            for c in range(8):
                                ins = tensor.matmul(
                                    sc_ps[gs % 2][:, 0:hi - lo],
                                    kt_buf[sp][:, c * D + kc * 128: c * D + kc * 128 + 128],
                                    qt_buf[qp][:, c * D + lo: c * D + hi],
                                    start=(c == 0), stop=(c == 7))
                            ins.then_inc(sem["sc_done"], 1)
                            gs += 1
                        if si == 0:
                            tensor.wait_ge(sem["vg0"], 16)
                        else:
                            v_cnt[sp] += 1
                            tensor.wait_ge(vvS[sp], 16 * v_cnt[sp])
                        tensor.wait_ge(sem["r_evac"], si)
                        waited_exp = 0
                        waited_mask = 0
                        for qt in range(8):
                            kcmax = qt if diag else 7
                            if gav >= 2:
                                tensor.wait_ge(sem["o_add"], gav - 1)
                            for kc in range(kcmax + 1):
                                need = sup["last"][kc] + 1
                                if need > waited_exp:
                                    tensor.wait_ge(sem["exp_done"], need)
                                    waited_exp = need
                                if diag and kc == qt:
                                    mneed = sup["masks"][kc] + 1
                                    if mneed > waited_mask:
                                        tensor.wait_ge(sem["vmask"], mneed)
                                        waited_mask = mneed
                                lhs = pt_buf[:, kc * D + qt * 128: kc * D + qt * 128 + 128]
                                for h in range(2):
                                    tensor.matmul(
                                        av_ps[qt % 2][:, h * 512:(h + 1) * 512], lhs,
                                        v_buf[sp][:, kc * 1032 + h * 512: kc * 1032 + (h + 1) * 512],
                                        start=(kc == 0), stop=(kc == kcmax))
                                ins = tensor.matmul(
                                    r_ps[:, qt:qt + 1], lhs,
                                    v_buf[sp][:, kc * 1032 + 1024: kc * 1032 + 1025],
                                    start=(kc == 0), stop=(kc == kcmax))
                            ins.then_inc(sem["av_done"], 1)
                            gav += 1

                @blk.scalar
                def _(scalar):
                    scalar.wait_ge(sem["ph1"], 5)
                    scalar.dma_start(
                        out=qt_buf[0][:].rearrange("p (c s) -> p c s", c=8),
                        in_=qT_h[:, :, 0:D].rearrange("c p s -> p c s")
                    ).then_inc(qtS[0], 16)
                    for si, sup in enumerate(SUPERS):
                        if si >= 1:
                            scalar.wait_ge(sem["av_done"], 8 * si)
                        for (kc, lo, hi, idx) in sup["steps"]:
                            scalar.wait_ge(sem["sc_done"], idx + 1)
                            scalar.activation(
                                pt_buf[:, kc * D + lo: kc * D + hi],
                                sc_ps[idx % 2][:, 0:hi - lo], AF.Exp,
                                bias=cpool[:, 2:3], scale=1.0 / 32.0
                            ).then_inc(sem["exp_done"], 1)

                @blk.vector
                def _(vector):
                    vector.wait_ge(sem["ph1"], 5)
                    for p in range(2):
                        for kc in range(8):
                            ins = vector.memset(
                                v_buf[p][:, kc * 1032 + 1024: kc * 1032 + 1032], 1.0)
                        ins.then_inc(sem["p2i"], 1)
                    o_cnt = 0
                    r_cnt = 0
                    for si, sup in enumerate(SUPERS):
                        qg, ks, diag = sup["qg"], sup["ks"], sup["diag"]
                        qp = qg % 2
                        if diag:
                            for kc in range(8):
                                vector.wait_ge(sem["exp_done"], sup["first"][kc] + 1)
                                col = kc * D + kc * 128
                                vector.tensor_mul(pt_buf[:, col:col + 128],
                                                  pt_buf[:, col:col + 128],
                                                  dmask_sb[:])\
                                    .then_inc(sem["vmask"], 1)
                        for qt in range(8):
                            vector.wait_ge(sem["av_done"], 8 * si + qt + 1)
                            if o_cnt:
                                vector.wait_ge(sem["o_add"], o_cnt)
                            dst = o_acc[qp][:, qt * D:(qt + 1) * D]
                            if ks == 0:
                                if qg >= 2 and qt == 0:
                                    vector.wait_ge(sem["vf"], 24 * (qg - 1))
                                vector.tensor_copy(dst, av_ps[qt % 2][:])\
                                    .then_inc(sem["o_add"], 1)
                            else:
                                vector.tensor_add(dst, dst, av_ps[qt % 2][:])\
                                    .then_inc(sem["o_add"], 1)
                            o_cnt += 1
                        vector.wait_ge(sem["av_done"], 8 * (si + 1))
                        if r_cnt:
                            vector.wait_ge(sem["r_evac"], r_cnt)
                        if ks == 0:
                            if qg >= 2:
                                vector.wait_ge(sem["rv"], qg - 1)
                            vector.tensor_copy(r_acc[qp][:], r_ps[:, 0:8])\
                                .then_inc(sem["r_evac"], 1)
                        else:
                            vector.tensor_add(r_acc[qp][:], r_acc[qp][:], r_ps[:, 0:8])\
                                .then_inc(sem["r_evac"], 1)
                        r_cnt += 1
                        if diag:
                            vector.wait_ge(sem["r_evac"], r_cnt)
                            vector.reciprocal(rinv[qp][:], r_acc[qp][:])\
                                .then_inc(sem["rv"], 1)
                        if diag and qg == QG - 1:
                            n_adds = 8 * (qg * (qg + 1) // 2 + qg + 1)
                            vf2c = 0
                            guse = 0
                            for qt in range(1, 8, 2):
                                guse += 1
                                rows = (qg * 8 + qt) * 128
                                vector.wait_ge(sem["o_add"], n_adds)
                                vector.wait_ge(sem["rv"], qg + 1)
                                vector.wait_ge(gS2[1], 16 * (4 * qg + guse))
                                vector.wait_ge(xrS[1], 16 * (4 * qg + guse))
                                vector.wait_ge(obS[1], 16 * (qg * 4 + qt // 2))
                                if vf2c:
                                    vector.wait_ge(sem["vf2"], vf2c)
                                vector.tensor_mul(out_sb[1][:],
                                                  o_acc[qp][:, qt * D:(qt + 1) * D],
                                                  g_buf[1][:]).then_inc(sem["vf2"], 1)
                                vf2c += 1
                                vector.wait_ge(sem["vf2"], vf2c)
                                rb = rinv[qp][:, qt:qt + 1].broadcast_to([128, D])
                                vector.tensor_mul(out_sb[1][:], out_sb[1][:], rb)\
                                    .then_inc(sem["vf2"], 1)
                                vf2c += 1
                                vector.wait_ge(sem["vf2"], vf2c)
                                vector.tensor_add(out_sb[1][:], out_sb[1][:],
                                                  x2_buf[1][:]).then_inc(sem["vf2"], 1)
                                vf2c += 1

                @blk.gpsimd
                def _(gpsimd):
                    gpsimd.wait_ge(sem["ph1"], 5)
                    gpsimd.wait_ge(sem["p2i"], 2)
                    gpsimd.dma_start(
                        out=v_buf[0][:].rearrange("p (t z) -> p t z", t=8)[:, :, 0:D],
                        in_=v_h[0:D, :].rearrange("(t p) d -> p t d", p=128)
                    ).then_inc(sem["vg0"], 16)
                    g_cnt = [0, 0]
                    ob_cnt = [0, 0]
                    vf_cnt = 0
                    for qg in range(QG):
                        qp = qg % 2
                        vf_base = qg * 24
                        n_adds = 8 * (qg * (qg + 1) // 2 + qg + 1)
                        for qt in range(8):
                            rows = (qg * 8 + qt) * 128
                            par = qt % 2
                            if qt < 2:
                                if g_cnt[par] > 0:
                                    gpsimd.wait_ge(sem["vf"],
                                                   vf_base - 24 + 3 * (6 + qt) + 3)
                                gpsimd.dma_start(out=g_buf[par][:],
                                                 in_=g_h[rows:rows + 128, :])\
                                    .then_inc(gS2[par], 16)
                                gpsimd.dma_start(out=x2_buf[par][:],
                                                 in_=x_d[rows:rows + 128, :])\
                                    .then_inc(xrS[par], 16)
                                g_cnt[par] += 1
                            split = (qg == QG - 1 and par == 1)
                            if not split:
                                gpsimd.wait_ge(sem["o_add"], n_adds)
                                gpsimd.wait_ge(sem["rv"], qg + 1)
                                gpsimd.wait_ge(gS2[par], 16 * g_cnt[par])
                                gpsimd.wait_ge(xrS[par], 16 * g_cnt[par])
                                if qg > 0 or qt >= 2:
                                    gpsimd.wait_ge(obS[par], 16 * (qg * 4 + qt // 2))
                                if vf_cnt:
                                    gpsimd.wait_ge(sem["vf"], vf_cnt)
                                gpsimd.tensor_mul(res_sb[:],
                                                  o_acc[qp][:, qt * D:(qt + 1) * D],
                                                  g_buf[par][:]).then_inc(sem["vf"], 1)
                                vf_cnt += 1
                                gpsimd.wait_ge(sem["vf"], vf_cnt)
                                rb = rinv[qp][:, qt:qt + 1].broadcast_to([128, D])
                                gpsimd.tensor_mul(res_sb[:], res_sb[:], rb)\
                                    .then_inc(sem["vf"], 1)
                                vf_cnt += 1
                                gpsimd.wait_ge(sem["vf"], vf_cnt)
                                gpsimd.tensor_add(out_sb[par][:], res_sb[:],
                                                  x2_buf[par][:]).then_inc(sem["vf"], 1)
                                vf_cnt += 1
                                gpsimd.wait_ge(sem["vf"], vf_cnt)
                            else:
                                gpsimd.wait_ge(sem["vf2"], 3 * ((qt + 1) // 2))
                            gpsimd.dma_start(out=out_d[rows:rows + 128, :],
                                             in_=out_sb[par][:])\
                                .then_inc(obS[par], 16)
                            ob_cnt[par] += 1
                            if qt + 2 < 8:
                                rows2 = (qg * 8 + qt + 2) * 128
                                if split:
                                    gpsimd.wait_ge(sem["vf2"], 3 * ((qt + 1) // 2))
                                gpsimd.dma_start(out=g_buf[par][:],
                                                 in_=g_h[rows2:rows2 + 128, :])\
                                    .then_inc(gS2[par], 16)
                                gpsimd.dma_start(out=x2_buf[par][:],
                                                 in_=x_d[rows2:rows2 + 128, :])\
                                    .then_inc(xrS[par], 16)
                                g_cnt[par] += 1
                    gpsimd.wait_ge(obS[0], 16 * ob_cnt[0])
                    gpsimd.wait_ge(obS[1], 16 * ob_cnt[1])

    return nc


def _consts():
    i = np.arange(D, dtype=np.float64)
    ang = 2.0 * np.pi * np.outer(i, i) / D
    ang = np.mod(ang, 2.0 * np.pi)
    cos = np.cos(ang)
    sin = np.sin(ang)
    return dict(
        cosm=cos.astype(BF),
        sinm=sin.astype(BF),
        sinnm=(-sin).astype(BF),
        ccm=(cos / D).astype(BF),
        csnm=(-sin / D).astype(BF),
        ident=np.eye(128, dtype=BF),
        dmask=(np.tril(np.ones((128, 128), np.float32)).T).astype(BF),
    )


def kernel(L0_attn_x, L0_attn_W_q, L0_attn_W_k, L0_attn_W_v,
           L0_ffn_w_freq_real, L0_ffn_w_freq_imag, L0_ffn_bias):
    S = 8192
    nc = build_nc(S)
    consts = _consts()
    shared = dict(
        wq=np.ascontiguousarray(L0_attn_W_q, dtype=np.float32),
        wk=np.ascontiguousarray(L0_attn_W_k, dtype=np.float32),
        wv=np.ascontiguousarray(L0_attn_W_v, dtype=np.float32),
        wfr=np.ascontiguousarray(L0_ffn_w_freq_real, dtype=np.float32),
        wfi=np.ascontiguousarray(L0_ffn_w_freq_imag, dtype=np.float32),
        fbias=np.ascontiguousarray(L0_ffn_bias, dtype=np.float32).reshape(1, D),
        **consts,
    )
    x = np.asarray(L0_attn_x, dtype=np.float32)
    in_maps = []
    for b in range(8):
        m = dict(x=np.ascontiguousarray(x[b]), **shared)
        in_maps.append(m)
    res = run_bass_kernel_spmd(nc, in_maps, core_ids=list(range(8)))
    out = np.stack([np.asarray(res.results[b]["out"]) for b in range(8)], axis=0)
    return out.astype(np.float32)


# version 20
# speedup vs baseline: 1.1713x; 1.0282x over previous
import numpy as np
import ml_dtypes
from contextlib import ExitStack

import concourse.bass as bass
import concourse.mybir as mybir
from concourse.bass_utils import run_bass_kernel_spmd

F32 = mybir.dt.float32
BF16 = mybir.dt.bfloat16
AF = mybir.ActivationFunctionType
ALU = mybir.AluOpType
AX = mybir.AxisListType
BF = ml_dtypes.bfloat16

D = 1024
NCHUNK = 8
EXP_SHIFT = -12.0


def p2_supers(S):
    QG = S // 1024
    supers = []
    step_idx = 0
    mask_idx = 0
    for qg in range(QG):
        for ks in range(qg + 1):
            diag = ks == qg
            steps = []
            first_step = {}
            last_step = {}
            for kc in range(8):
                lo = kc * 128 if diag else 0
                subs = []
                if lo < 512:
                    subs = [(lo, 512), (512, 1024)]
                else:
                    subs = [(lo, 1024)]
                for (a, b) in subs:
                    if kc not in first_step:
                        first_step[kc] = step_idx
                    last_step[kc] = step_idx
                    steps.append((kc, a, b, step_idx))
                    step_idx += 1
            masks = {}
            if diag:
                for kc in range(8):
                    masks[kc] = mask_idx
                    mask_idx += 1
            supers.append(dict(qg=qg, ks=ks, diag=diag, steps=steps,
                               first=first_step, last=last_step, masks=masks,
                               step_end=step_idx))
    return supers


def build_nc(S=8192, n_cores=8):
    G = S // 512
    QG = S // 1024
    SUPERS = p2_supers(S)
    NSUP = len(SUPERS)

    nc = bass.Bass("TRN2", target_bir_lowering=False, debug=False)

    x_d = nc.dram_tensor("x", [S, D], F32, kind="ExternalInput").ap()
    wq_d = nc.dram_tensor("wq", [D, D], F32, kind="ExternalInput").ap()
    wk_d = nc.dram_tensor("wk", [D, D], F32, kind="ExternalInput").ap()
    wv_d = nc.dram_tensor("wv", [D, D], F32, kind="ExternalInput").ap()
    wfr_d = nc.dram_tensor("wfr", [D, D], F32, kind="ExternalInput").ap()
    wfi_d = nc.dram_tensor("wfi", [D, D], F32, kind="ExternalInput").ap()
    fb_d = nc.dram_tensor("fbias", [1, D], F32, kind="ExternalInput").ap()
    cos_d = nc.dram_tensor("cosm", [D, D], BF16, kind="ExternalInput").ap()
    sin_d = nc.dram_tensor("sinm", [D, D], BF16, kind="ExternalInput").ap()
    sinn_d = nc.dram_tensor("sinnm", [D, D], BF16, kind="ExternalInput").ap()
    cc_d = nc.dram_tensor("ccm", [D, D], BF16, kind="ExternalInput").ap()
    csn_d = nc.dram_tensor("csnm", [D, D], BF16, kind="ExternalInput").ap()
    id_d = nc.dram_tensor("ident", [128, 128], BF16, kind="ExternalInput").ap()
    dm_d = nc.dram_tensor("dmask", [128, 128], BF16, kind="ExternalInput").ap()
    out_d = nc.dram_tensor("out", [S, D], F32, kind="ExternalOutput").ap()

    qT_h = nc.dram_tensor("qT_h", [8, 128, S], BF16, kind="Internal").ap()
    kT_h = nc.dram_tensor("kT_h", [8, 128, S], BF16, kind="Internal").ap()
    v_h = nc.dram_tensor("v_h", [S, D], BF16, kind="Internal").ap()
    g_h = nc.dram_tensor("g_h", [S, D], F32, kind="Internal").ap()
    xn_h = nc.dram_tensor("xn_h", [S, D], BF16, kind="Internal").ap()

    with ExitStack() as outer:
        sem = {}
        for name in (["dGa", "dS1", "ph0", "ph1",
                     "t_p0", "sc_p0",
                     "x0", "x1", "qS0", "kS0", "vS0", "gS",
                     "vln", "s_ln", "t_qkvl", "v_ev", "v_ev2", "sc_sil", "rv",
                     "xnst0", "xnst1", "xnl0", "xnl1",
                     "sc_done", "exp_done", "vmask", "av_done", "o_add",
                     "r_evac", "vf", "p2i",
                     "kt0", "kt1", "vv0", "vv1", "qt0", "qt1",
                     "gg0", "gg1", "gx0", "gx1", "go0", "go1"]
                     + [f"dGq{i}" for i in range(8)]
                     + [f"dGk{i}" for i in range(8)]
                     + [f"dGv{i}" for i in range(8)]
                     + ["dStrg", "vg0", "vf2"]
                     + [f"dGr{i}" for i in range(8)] + [f"dGi{i}" for i in range(8)]):
            sem[name] = outer.enter_context(nc.semaphore(name))

        def osb(name, shape, dt):
            return outer.enter_context(nc.sbuf_tensor(name, shape, dt))
        cpool = osb("cpool", [128, 4], F32)
        ones_row = osb("ones_row", [1, 128], BF16)
        ident_sb = osb("ident_sb", [128, 128], BF16)
        dmask_sb = osb("dmask_sb", [128, 128], BF16)
        mil_sb = osb("mil_sb", [128, 4], mybir.dt.int32)

        st01 = ExitStack()

        def osb01(name, shape, dt):
            return st01.enter_context(nc.sbuf_tensor(name, shape, dt))
        m_sb = osb01("m_sb", [128, 8 * D], BF16)
        bias_sb = osb01("bias_sb", [1, D], BF16)
        x_buf0 = osb01("x_buf0", [128, 4 * D], F32)
        stats = osb01("stats", [128, 32], F32)
        xc4 = [osb01(f"xc4_{i}", [128, D], F32) for i in range(4)]
        xsq_sb = osb01("xsq_sb", [128, D], F32)
        xn4 = [osb01(f"xn4_{i}", [128, D], BF16) for i in range(4)]
        xnT = [osb01(f"xnT{i}", [128, 8 * 512], BF16) for i in range(2)]
        wq_sb = osb01("wq_sb", [128, 8 * D], BF16)
        wk_sb = osb01("wk_sb", [128, 8 * D], BF16)
        wv_sb = osb01("wv_sb", [128, 8 * D], BF16)

        with ExitStack() as st0:
            def sb0(name, shape, dt):
                return st0.enter_context(nc.sbuf_tensor(name, shape, dt))
            wr_sb = sb0("wr_sb", [128, 8 * D], BF16)
            wi_sb = sb0("wi_sb", [128, 8 * D], BF16)
            trgC = [sb0(f"trgC{i}", [128, 8 * 128], BF16) for i in range(2)]
            trgS = [sb0(f"trgS{i}", [128, 8 * 128], BF16) for i in range(2)]
            trgSN = [sb0(f"trgSN{i}", [128, 8 * 128], BF16) for i in range(2)]
            trgCC = sb0("trgCC", [128, 8 * D], BF16)
            trgCSN = sb0("trgCSN", [128, 8 * D], BF16)
            art_sb = sb0("art_sb", [128, 8 * 128], BF16)
            ait_sb = sb0("ait_sb", [128, 8 * 128], BF16)
            pp = st0.enter_context(nc.psum_tensor("pp", [128, 2048], F32))

            def p0_groups(b):
                bp = b % 2
                gs = []
                for j in range(8):
                    gs.append(("art", j, (wr_sb, trgC[bp]), (wi_sb, trgS[bp])))
                for j in range(8):
                    gs.append(("ait", j, (wi_sb, trgC[bp]), (wr_sb, trgSN[bp])))
                for ub in range(2):
                    gs.append(("m", ub, (art_sb, trgCC), (ait_sb, trgCSN)))
                return gs

            def trg_load(sync, b):
                for srcd, dst in [(cos_d, trgC), (sin_d, trgS), (sinn_d, trgSN)]:
                    sync.dma_start(
                        out=dst[b % 2][:].rearrange("p (c e) -> p c e", c=8),
                        in_=srcd[:, b * 128:(b + 1) * 128]
                        .rearrange("(c p) e -> p c e", p=128)
                    ).then_inc(sem["dStrg"], 16)

            with nc.Block() as blk:
                @blk.sync
                def _(sync):
                    trg_load(sync, 0)
                    trg_load(sync, 1)
                    sync.dma_start(
                        out=x_buf0[:].rearrange("p (t d) -> p t d", t=4),
                        in_=x_d[0:512, :].rearrange("(t p) d -> p t d", p=128)
                    ).then_inc(sem["x0"], 16)
                    sync.dma_start(out=ident_sb[:], in_=id_d).then_inc(sem["dS1"], 16)
                    sync.dma_start(out=dmask_sb[:], in_=dm_d).then_inc(sem["dS1"], 16)
                    for srcd, dst in [(cc_d, trgCC), (csn_d, trgCSN)]:
                        sync.dma_start(
                            out=dst[:].rearrange("p (c e) -> p c e", c=8),
                            in_=srcd.rearrange("(c p) e -> p c e", p=128)
                        ).then_inc(sem["dS1"], 16)
                    for b in range(2, 8):
                        sync.wait_ge(sem["t_p0"], 18 * (b - 2) + 16)
                        trg_load(sync, b)
                        if b == 4:
                            for t in range(4):
                                sync.wait_ge(sem["vln"], 21 + t)
                                sync.dma_start(out=xn_h[t * 128:(t + 1) * 128, :],
                                               in_=xn4[t][:]).then_inc(sem["xnst0"], 16)
                        if b == 5:
                            sync.wait_ge(sem["xnst0"], 64)
                            for c in range(4):
                                sync.dma_start_transpose(
                                    xnT[0][:, c * 512:(c + 1) * 512],
                                    xn_h[0:512, c * 128:(c + 1) * 128]
                                ).then_inc(sem["xnl0"], 16)
                        if b == 6:
                            for c in range(4, 8):
                                sync.dma_start_transpose(
                                    xnT[0][:, c * 512:(c + 1) * 512],
                                    xn_h[0:512, c * 128:(c + 1) * 128]
                                ).then_inc(sem["xnl0"], 16)
                    sync.wait_ge(sem["dStrg"], 48 * 8)
                    sync.wait_ge(sem["dS1"], 64)
                    sync.nop().then_inc(sem["ph0"], 1)

                @blk.gpsimd
                def _(gpsimd):
                    for srcd, dst, pref in [(wfr_d, wr_sb, "dGr"), (wfi_d, wi_sb, "dGi")]:
                        for j in range(8):
                            gpsimd.dma_start(
                                out=dst[:].rearrange("p (c e) -> p c e", c=8)
                                [:, :, j * 128:(j + 1) * 128],
                                in_=srcd.rearrange("(c p) e -> p c e", p=128)
                                [:, :, j * 128:(j + 1) * 128]
                            ).then_inc(sem[f"{pref}{j}"], 16)
                    for srcd, dst, pref in [(wq_d, wq_sb, "dGq"), (wk_d, wk_sb, "dGk")]:
                        for e in range(8):
                            gpsimd.dma_start(
                                out=dst[:].rearrange("p (c e) -> p c e", c=8)
                                [:, :, e * 128:(e + 1) * 128],
                                in_=srcd.rearrange("(c p) e -> p c e", p=128)
                                [:, :, e * 128:(e + 1) * 128]
                            ).then_inc(sem[f"{pref}{e}"], 16)
                    gpsimd.dma_start(out=bias_sb[:], in_=fb_d).then_inc(sem["dGa"], 16)
                    for e in range(8):
                        gpsimd.dma_start(
                            out=wv_sb[:].rearrange("p (c e) -> p c e", c=8)
                            [:, :, e * 128:(e + 1) * 128],
                            in_=wv_d.rearrange("(c p) e -> p c e", p=128)
                            [:, :, e * 128:(e + 1) * 128]
                        ).then_inc(sem[f"dGv{e}"], 16)
                    gpsimd.nop().then_inc(sem["ph0"], 1)

                @blk.vector
                def _(vector):
                    vector.memset(ones_row[:], 1.0)
                    vector.memset(cpool[:, 0:1], 1e-5)
                    vector.memset(cpool[:, 1:2], 0.0)
                    vector.memset(cpool[:, 2:3], EXP_SHIFT)
                    vcnt = 0
                    vector.wait_ge(sem["x0"], 16)
                    for t in range(4):
                        xt = x_buf0[:, t * D:(t + 1) * D]
                        c0 = 8 * t
                        if vcnt:
                            vector.wait_ge(sem["vln"], vcnt)
                        vector.reduce_sum(stats[:, c0:c0 + 1], xt, axis=AX.X)\
                            .then_inc(sem["vln"], 1); vcnt += 1
                        vector.wait_ge(sem["vln"], vcnt)
                        vector.tensor_scalar_mul(stats[:, c0 + 1:c0 + 2],
                                                 stats[:, c0:c0 + 1], -1.0 / D)\
                            .then_inc(sem["vln"], 1); vcnt += 1
                        vector.wait_ge(sem["vln"], vcnt)
                        vector.tensor_scalar_add(xc4[t][:], xt,
                                                 stats[:, c0 + 1:c0 + 2])\
                            .then_inc(sem["vln"], 1); vcnt += 1
                        vector.wait_ge(sem["vln"], vcnt)
                        vector.tensor_mul(xsq_sb[:], xc4[t][:], xc4[t][:])\
                            .then_inc(sem["vln"], 1); vcnt += 1
                        vector.wait_ge(sem["vln"], vcnt)
                        vector.reduce_sum(stats[:, c0 + 2:c0 + 3], xsq_sb[:],
                                          axis=AX.X)\
                            .then_inc(sem["vln"], 1); vcnt += 1
                    for t in range(4):
                        vector.wait_ge(sem["s_ln"], 2 * t + 2)
                        vector.wait_ge(sem["vln"], vcnt)
                        vector.tensor_scalar_mul(xn4[t][:], xc4[t][:],
                                                 stats[:, 8 * t + 4:8 * t + 5])\
                            .then_inc(sem["vln"], 1); vcnt += 1
                    vector.memset(mil_sb[:, 0:1], 0).then_inc(sem["ph0"], 1)

                @blk.tensor
                def _(tensor):
                    for b in range(8):
                        for g18, (kind, j, (l1, r1), (l2, r2)) in enumerate(p0_groups(b)):
                            gi = 18 * b + g18
                            if g18 == 0:
                                tensor.wait_ge(sem["dStrg"], 48 * (b + 1))
                            if b == 0 and kind == "art":
                                tensor.wait_ge(sem[f"dGr{j}"], 16)
                            if b == 0 and kind == "ait":
                                tensor.wait_ge(sem[f"dGi{j}"], 16)
                                tensor.wait_ge(sem[f"dGr{j}"], 16)
                            if g18 == 16:
                                tensor.wait_ge(sem["sc_p0"], 18 * b + 16)
                                if b == 0:
                                    tensor.wait_ge(sem["dS1"], 64)
                            if gi >= 4:
                                tensor.wait_ge(sem["sc_p0"], gi - 3)
                            ncols = 128 if kind != "m" else 512
                            ps = pp[:, (gi % 4) * 512:(gi % 4) * 512 + ncols]
                            for c in range(8):
                                if kind == "m":
                                    lhs = l1[:, c * 128: c * 128 + 128]
                                    rhs = r1[:, c * D + j * 512: c * D + j * 512 + 512]
                                else:
                                    lhs = l1[:, c * D + j * 128: c * D + j * 128 + 128]
                                    rhs = r1[:, c * 128:(c + 1) * 128]
                                tensor.matmul(ps, lhs, rhs, start=(c == 0), stop=False)
                            if b == 0 and kind == "art":
                                tensor.wait_ge(sem[f"dGi{j}"], 16)
                            for c in range(8):
                                if kind == "m":
                                    lhs = l2[:, c * 128: c * 128 + 128]
                                    rhs = r2[:, c * D + j * 512: c * D + j * 512 + 512]
                                else:
                                    lhs = l2[:, c * D + j * 128: c * D + j * 128 + 128]
                                    rhs = r2[:, c * 128:(c + 1) * 128]
                                ins = tensor.matmul(ps, lhs, rhs, start=False, stop=(c == 7))
                            ins.then_inc(sem["t_p0"], 1)
                    tensor.nop().then_inc(sem["ph0"], 1)

                @blk.scalar
                def _(scalar):
                    def p0_copies(b):
                        for g18, (kind, j, _, _2) in enumerate(p0_groups(b)):
                            gi = 18 * b + g18
                            scalar.wait_ge(sem["t_p0"], gi + 1)
                            if kind == "m":
                                dst = m_sb[:, b * D + j * 512: b * D + j * 512 + 512]
                                ps = pp[:, (gi % 4) * 512:(gi % 4) * 512 + 512]
                            else:
                                dst = (art_sb if kind == "art" else ait_sb)[:, j * 128: j * 128 + 128]
                                ps = pp[:, (gi % 4) * 512:(gi % 4) * 512 + 128]
                            scalar.copy(dst, ps).then_inc(sem["sc_p0"], 1)
                    for b in range(2):
                        p0_copies(b)
                    for t in range(4):
                        c0 = 8 * t
                        scalar.wait_ge(sem["vln"], 5 * t + 5)
                        if t >= 1:
                            scalar.wait_ge(sem["s_ln"], 2 * t)
                        scalar.activation(stats[:, c0 + 3:c0 + 4],
                                          stats[:, c0 + 2:c0 + 3], AF.Ln,
                                          bias=cpool[:, 0:1], scale=1.0 / D)\
                            .then_inc(sem["s_ln"], 1)
                        scalar.wait_ge(sem["s_ln"], 2 * t + 1)
                        scalar.activation(stats[:, c0 + 4:c0 + 5],
                                          stats[:, c0 + 3:c0 + 4], AF.Exp,
                                          bias=cpool[:, 1:2], scale=-0.5)\
                            .then_inc(sem["s_ln"], 1)
                    for b in range(2, 8):
                        p0_copies(b)
                    scalar.memzero(mil_sb[:, 1:2]).then_inc(sem["ph0"], 1)

        with ExitStack() as st1:
            def sb1(name, shape, dt):
                return st1.enter_context(nc.sbuf_tensor(name, shape, dt))
            x_buf = [x_buf0, sb1("x_buf1", [128, 4 * D], F32)]
            qstage = sb1("qstage", [128, 8 * 512], BF16)
            kstage = sb1("kstage", [128, 8 * 512], BF16)
            vstage = sb1("vstage", [128, 4 * D], BF16)
            gstage = sb1("gstage", [128, 4 * D], F32)
            lin_stage = sb1("lin_stage", [128, 4 * D], F32)
            qk_ps = [st1.enter_context(nc.psum_tensor(f"qk_ps{i}", [128, 512], F32))
                     for i in range(4)]
            v_ps = st1.enter_context(nc.psum_tensor("v_ps", [128, 1024], F32))
            l_ps = st1.enter_context(nc.psum_tensor("l_ps", [128, 1024], F32))

            xsem = [sem["x0"], sem["x1"]]
            qS = sem["qS0"]
            kS = sem["kS0"]
            vS = sem["vS0"]

            def vln_val(g, t, k):
                return 24 * g + 6 * t + k

            with nc.Block() as blk:
                @blk.sync
                def _(sync):
                    sync.wait_ge(sem["ph0"], 5)
                    xnstS = [sem["xnst0"], sem["xnst1"]]
                    xnlS = [sem["xnl0"], sem["xnl1"]]
                    for g in range(G):
                        if g >= 1:
                            for t in range(4):
                                ti = 4 * g + t
                                sync.wait_ge(sem["vln"], 24 * g + 21 + t)
                                sync.dma_start(out=xn_h[ti * 128:(ti + 1) * 128, :],
                                               in_=xn4[ti % 4][:])\
                                    .then_inc(xnstS[g % 2], 16)
                        if g >= 1:
                            sync.wait_ge(xnstS[g % 2], 64 * (g // 2 + 1))
                            if g >= 2:
                                sync.wait_ge(sem["t_qkvl"], 32 * (g - 1))
                            for c in range(8):
                                sync.dma_start_transpose(
                                    xnT[g % 2][:, c * 512:(c + 1) * 512],
                                    xn_h[g * 512:(g + 1) * 512, c * 128:(c + 1) * 128]
                                ).then_inc(xnlS[g % 2], 16)
                        if g == 0 and G > 1:
                            sync.dma_start(
                                out=x_buf[1][:].rearrange("p (t d) -> p t d", t=4),
                                in_=x_d[512:1024, :].rearrange("(t p) d -> p t d", p=128)
                            ).then_inc(xsem[1], 16)
                        if g + 2 < G:
                            sync.wait_ge(sem["vln"], 24 * (g + 1))
                            sync.dma_start(
                                out=x_buf[g % 2][:].rearrange("p (t d) -> p t d", t=4),
                                in_=x_d[(g + 2) * 512:(g + 3) * 512, :]
                                .rearrange("(t p) d -> p t d", p=128)
                            ).then_inc(xsem[g % 2], 16)
                    sync.wait_ge(qS, 16 * G)
                    sync.wait_ge(kS, 16 * G)
                    sync.wait_ge(vS, 16 * G)
                    sync.wait_ge(sem["gS"], 16 * G)
                    sync.nop().then_inc(sem["ph1"], 1)

                @blk.vector
                def _(vector):
                    vector.wait_ge(sem["ph0"], 5)
                    vcnt = 24
                    for g in range(G + 1):
                        if 1 <= g < G:
                            vector.wait_ge(xsem[g % 2], 16 * (g // 2 + 1))
                            for t in range(4):
                                xt = x_buf[g % 2][:, t * D:(t + 1) * D]
                                ti = 4 * g + t
                                c0 = 8 * t
                                if g >= 1:
                                    vector.wait_ge(sem["s_ln"], 2 * (4 * (g - 1) + t) + 2)
                                if vcnt:
                                    vector.wait_ge(sem["vln"], vcnt)
                                vector.reduce_sum(stats[:, c0:c0 + 1], xt, axis=AX.X)\
                                    .then_inc(sem["vln"], 1); vcnt += 1
                                vector.wait_ge(sem["vln"], vcnt)
                                vector.tensor_scalar_mul(stats[:, c0 + 1:c0 + 2],
                                                         stats[:, c0:c0 + 1], -1.0 / D)\
                                    .then_inc(sem["vln"], 1); vcnt += 1
                                vector.wait_ge(sem["vln"], vcnt)
                                vector.tensor_scalar_add(xc4[t][:], xt,
                                                         stats[:, c0 + 1:c0 + 2])\
                                    .then_inc(sem["vln"], 1); vcnt += 1
                                vector.wait_ge(sem["vln"], vcnt)
                                vector.tensor_mul(xsq_sb[:], xc4[t][:], xc4[t][:])\
                                    .then_inc(sem["vln"], 1); vcnt += 1
                                vector.wait_ge(sem["vln"], vcnt)
                                vector.reduce_sum(stats[:, c0 + 2:c0 + 3], xsq_sb[:],
                                                  axis=AX.X)\
                                    .then_inc(sem["vln"], 1); vcnt += 1
                            for t in range(4):
                                ti = 4 * g + t
                                vector.wait_ge(sem["s_ln"], 2 * ti + 2)
                                if g >= 1:
                                    xnstS = [sem["xnst0"], sem["xnst1"]]
                                    vector.wait_ge(xnstS[(g - 1) % 2],
                                                   64 * ((g - 1) // 2 + 1))
                                vector.wait_ge(sem["vln"], vcnt)
                                vector.tensor_scalar_mul(xn4[ti % 4][:], xc4[t][:],
                                                         stats[:, 8 * t + 4:8 * t + 5])\
                                    .then_inc(sem["vln"], 1); vcnt += 1
                        if g >= 1:
                            gg = g - 1
                            for t in range(4):
                                if t == 0 and gg >= 1:
                                    vector.wait_ge(vS, 16 * gg)
                                for k, (src_ps, dst, h) in enumerate(
                                        [(v_ps, vstage, 0), (l_ps, lin_stage, 0),
                                         (v_ps, vstage, 1), (l_ps, lin_stage, 1)]):
                                    vector.wait_ge(sem["t_qkvl"],
                                                   32 * gg + 16 + 4 * t + k + 1)
                                    if k == 1 and gg >= 1:
                                        vector.wait_ge(sem["sc_sil"],
                                                       4 * (gg - 1) + t + 1)
                                    vector.tensor_copy(
                                        dst[:, t * D + h * 512: t * D + h * 512 + 512],
                                        src_ps[:, h * 512:(h + 1) * 512])\
                                        .then_inc(sem["v_ev2"], 1)
                    vector.memset(mil_sb[:, 0:1], 0).then_inc(sem["ph1"], 1)

                @blk.scalar
                def _(scalar):
                    scalar.wait_ge(sem["ph0"], 5)
                    for g in range(G + 1):
                        if g >= 1:
                            gg = g - 1
                            for j in range(16):
                                scalar.wait_ge(sem["t_qkvl"], 32 * gg + j + 1)
                                if j == 0 and gg >= 1:
                                    scalar.wait_ge(qS, 16 * gg)
                                if j == 8 and gg >= 1:
                                    scalar.wait_ge(kS, 16 * gg)
                                stg = qstage if j < 8 else kstage
                                e = j % 8
                                scalar.copy(stg[:, e * 512:(e + 1) * 512],
                                            qk_ps[j % 4][:])\
                                    .then_inc(sem["v_ev"], 1)
                            scalar.wait_ge(sem["v_ev"], 16 * gg + 16)
                            scalar.dma_start(
                                out=qT_h[:, :, gg * 512:(gg + 1) * 512]
                                .rearrange("c p s -> p c s"),
                                in_=qstage[:].rearrange("p (c s) -> p c s", c=8)
                            ).then_inc(qS, 16)
                            scalar.dma_start(
                                out=kT_h[:, :, gg * 512:(gg + 1) * 512]
                                .rearrange("c p s -> p c s"),
                                in_=kstage[:].rearrange("p (c s) -> p c s", c=8)
                            ).then_inc(kS, 16)
                        if 1 <= g < G:
                            for t in range(4):
                                ti = 4 * g + t
                                c0 = 8 * t
                                scalar.wait_ge(sem["vln"], 24 * g + 5 * t + 5)
                                if ti >= 1:
                                    scalar.wait_ge(sem["s_ln"], 2 * ti)
                                scalar.activation(stats[:, c0 + 3:c0 + 4],
                                                  stats[:, c0 + 2:c0 + 3], AF.Ln,
                                                  bias=cpool[:, 0:1], scale=1.0 / D)\
                                    .then_inc(sem["s_ln"], 1)
                                scalar.wait_ge(sem["s_ln"], 2 * ti + 1)
                                scalar.activation(stats[:, c0 + 4:c0 + 5],
                                                  stats[:, c0 + 3:c0 + 4], AF.Exp,
                                                  bias=cpool[:, 1:2], scale=-0.5)\
                                    .then_inc(sem["s_ln"], 1)
                        if g >= 1:
                            gg = g - 1
                            for t in range(4):
                                scalar.wait_ge(sem["v_ev2"], 16 * gg + 4 * t + 4)
                                if t == 0 and gg >= 1:
                                    scalar.wait_ge(sem["gS"], 16 * gg)
                                scalar.activation(gstage[:, t * D:(t + 1) * D],
                                                  lin_stage[:, t * D:(t + 1) * D],
                                                  AF.Silu).then_inc(sem["sc_sil"], 1)
                            scalar.wait_ge(sem["v_ev2"], 16 * gg + 16)
                            scalar.wait_ge(sem["sc_sil"], 4 * gg + 4)
                            scalar.dma_start(
                                out=v_h[gg * 512:(gg + 1) * 512, :]
                                .rearrange("(t p) d -> p t d", p=128),
                                in_=vstage[:].rearrange("p (t d) -> p t d", t=4)
                            ).then_inc(vS, 16)
                            scalar.dma_start(
                                out=g_h[gg * 512:(gg + 1) * 512, :]
                                .rearrange("(t p) d -> p t d", p=128),
                                in_=gstage[:].rearrange("p (t d) -> p t d", t=4)
                            ).then_inc(sem["gS"], 16)
                    scalar.memzero(mil_sb[:, 1:2]).then_inc(sem["ph1"], 1)

                @blk.tensor
                def _(tensor):
                    tensor.wait_ge(sem["ph0"], 5)
                    tensor.wait_ge(sem["dGa"], 16)
                    xnlS = [sem["xnl0"], sem["xnl1"]]
                    for g in range(G):
                        tensor.wait_ge(xnlS[g % 2], 128 * (g // 2 + 1))
                        for j in range(16):
                            if g == 0:
                                if j < 8:
                                    tensor.wait_ge(sem[f"dGq{j}"], 16)
                                else:
                                    tensor.wait_ge(sem[f"dGk{j % 8}"], 16)
                            w = wq_sb if j < 8 else wk_sb
                            e = j % 8
                            qj = 16 * g + j
                            if qj >= 4:
                                tensor.wait_ge(sem["v_ev"], qj - 3)
                            for c in range(8):
                                ins = tensor.matmul(
                                    qk_ps[qj % 4][:],
                                    w[:, c * D + e * 128: c * D + e * 128 + 128],
                                    xnT[g % 2][:, c * 512:(c + 1) * 512],
                                    start=(c == 0), stop=(c == 7))
                            ins.then_inc(sem["t_qkvl"], 1)
                        if g == 0:
                            for e in range(8):
                                tensor.wait_ge(sem[f"dGv{e}"], 16)
                        for t in range(4):
                            lhs = [xnT[g % 2][:, c * 512 + t * 128: c * 512 + t * 128 + 128]
                                   for c in range(8)]
                            base = 16 * g + 4 * (t - 1)
                            for k, (dst_ps, w, h, is_lin) in enumerate(
                                    [(v_ps, wv_sb, 0, False), (l_ps, m_sb, 0, True),
                                     (v_ps, wv_sb, 1, False), (l_ps, m_sb, 1, True)]):
                                if t > 0 or g > 0:
                                    tensor.wait_ge(sem["v_ev2"], max(0, base + k + 1))
                                for c in range(8):
                                    ins = tensor.matmul(
                                        dst_ps[:, h * 512:(h + 1) * 512], lhs[c],
                                        w[:, c * D + h * 512: c * D + (h + 1) * 512],
                                        start=(c == 0),
                                        stop=(c == 7 and not is_lin))
                                if is_lin:
                                    ins = tensor.matmul(
                                        dst_ps[:, h * 512:(h + 1) * 512],
                                        ones_row[0:1, :],
                                        bias_sb[0:1, h * 512:(h + 1) * 512],
                                        start=False, stop=True)
                                ins.then_inc(sem["t_qkvl"], 1)
                    tensor.nop().then_inc(sem["ph1"], 1)

                @blk.gpsimd
                def _(gpsimd):
                    gpsimd.wait_ge(sem["ph0"], 5)
                    gpsimd.nop().then_inc(sem["ph1"], 1)

        st01.close()
        with ExitStack() as st2:
            def sb2(name, shape, dt):
                return st2.enter_context(nc.sbuf_tensor(name, shape, dt))
            qt_buf = [sb2(f"qt_buf{i}", [128, 8 * D], BF16) for i in range(2)]
            kt_buf = [sb2(f"kt_buf{i}", [128, 8 * D], BF16) for i in range(2)]
            v_buf = [sb2(f"v_buf{i}", [128, 8 * 1032], BF16) for i in range(2)]
            pt_buf = sb2("pt_buf", [128, 8 * D], BF16)
            o_acc = [sb2(f"o_acc{i}", [128, 8 * D], F32) for i in range(2)]
            r_acc = [sb2(f"r_acc{i}", [128, 8], F32) for i in range(2)]
            rinv = [sb2(f"rinv{i}", [128, 8], F32) for i in range(2)]
            g_buf = [sb2(f"g_buf{i}", [128, D], F32) for i in range(2)]
            x2_buf = [sb2(f"x2_buf{i}", [128, D], F32) for i in range(2)]
            res_sb = sb2("res_sb", [128, D], F32)
            out_sb = [sb2(f"out_sb{i}", [128, D], F32) for i in range(2)]
            sc_ps = [st2.enter_context(nc.psum_tensor(f"sc_ps{i}", [128, 512], F32))
                     for i in range(2)]
            av_ps = [st2.enter_context(nc.psum_tensor(f"av_ps{i}", [128, 1024], F32))
                     for i in range(2)]
            r_ps = st2.enter_context(nc.psum_tensor("r_ps", [128, 8], F32))

            ktS = [sem["kt0"], sem["kt1"]]
            vvS = [sem["vv0"], sem["vv1"]]
            qtS = [sem["qt0"], sem["qt1"]]
            gS2 = [sem["gg0"], sem["gg1"]]
            xrS = [sem["gx0"], sem["gx1"]]
            obS = [sem["go0"], sem["go1"]]

            def sup_base(si):
                return 0 if si == 0 else SUPERS[si - 1]["step_end"]

            with nc.Block() as blk:
                @blk.sync
                def _(sync):
                    sync.wait_ge(sem["ph1"], 5)
                    kt_cnt = [0, 0]
                    v_cnt = [0, 0]
                    qt_cnt = [0, 0]
                    si = 0
                    for qg in range(QG):
                        qp = qg % 2
                        if qg >= 2:
                            prev_last = sum(range(1, qg)) - 1
                            sync.wait_ge(sem["sc_done"], SUPERS[prev_last]["step_end"])
                        if qg == 0:
                            qt_cnt[qp] += 1
                        else:
                            sync.dma_start(
                                out=qt_buf[qp][:].rearrange("p (c s) -> p c s", c=8),
                                in_=qT_h[:, :, qg * D:(qg + 1) * D].rearrange("c p s -> p c s")
                            ).then_inc(qtS[qp], 16)
                            qt_cnt[qp] += 1
                        for ks in range(qg + 1):
                            sp = si % 2
                            if si >= 2:
                                sync.wait_ge(sem["sc_done"], SUPERS[si - 2]["step_end"])
                            sync.dma_start(
                                out=kt_buf[sp][:].rearrange("p (c s) -> p c s", c=8),
                                in_=kT_h[:, :, ks * D:(ks + 1) * D].rearrange("c p s -> p c s")
                            ).then_inc(ktS[sp], 16)
                            kt_cnt[sp] += 1
                            if si >= 2:
                                sync.wait_ge(sem["av_done"], 8 * (si - 1))
                            if si == 0:
                                v_cnt[sp] += 1
                            else:
                                if si < 2:
                                    sync.wait_ge(sem["p2i"], 2)
                                sync.dma_start(
                                    out=v_buf[sp][:].rearrange("p (t z) -> p t z", t=8)[:, :, 0:D],
                                    in_=v_h[ks * D:(ks + 1) * D, :].rearrange("(t p) d -> p t d", p=128)
                                ).then_inc(vvS[sp], 16)
                                v_cnt[sp] += 1
                            si += 1

                @blk.tensor
                def _(tensor):
                    tensor.wait_ge(sem["ph1"], 5)
                    kt_cnt = [0, 0]
                    v_cnt = [0, 0]
                    qt_cnt = [0, 0]
                    gs = 0
                    gav = 0
                    for si, sup in enumerate(SUPERS):
                        qg, ks, diag = sup["qg"], sup["ks"], sup["diag"]
                        sp = si % 2
                        qp = qg % 2
                        kt_cnt[sp] += 1
                        tensor.wait_ge(ktS[sp], 16 * kt_cnt[sp])
                        if ks == 0:
                            qt_cnt[qp] += 1
                            tensor.wait_ge(qtS[qp], 16 * qt_cnt[qp])
                        for (kc, lo, hi, idx) in sup["steps"]:
                            if gs >= 2:
                                tensor.wait_ge(sem["exp_done"], gs - 1)
                            for c in range(8):
                                ins = tensor.matmul(
                                    sc_ps[gs % 2][:, 0:hi - lo],
                                    kt_buf[sp][:, c * D + kc * 128: c * D + kc * 128 + 128],
                                    qt_buf[qp][:, c * D + lo: c * D + hi],
                                    start=(c == 0), stop=(c == 7))
                            ins.then_inc(sem["sc_done"], 1)
                            gs += 1
                        if si == 0:
                            tensor.wait_ge(sem["vg0"], 16)
                        else:
                            v_cnt[sp] += 1
                            tensor.wait_ge(vvS[sp], 16 * v_cnt[sp])
                        tensor.wait_ge(sem["r_evac"], si)
                        waited_exp = 0
                        waited_mask = 0
                        for qt in range(8):
                            kcmax = qt if diag else 7
                            if gav >= 2:
                                tensor.wait_ge(sem["o_add"], gav - 1)
                            for kc in range(kcmax + 1):
                                need = sup["last"][kc] + 1
                                if need > waited_exp:
                                    tensor.wait_ge(sem["exp_done"], need)
                                    waited_exp = need
                                if diag and kc == qt:
                                    mneed = sup["masks"][kc] + 1
                                    if mneed > waited_mask:
                                        tensor.wait_ge(sem["vmask"], mneed)
                                        waited_mask = mneed
                                lhs = pt_buf[:, kc * D + qt * 128: kc * D + qt * 128 + 128]
                                for h in range(2):
                                    tensor.matmul(
                                        av_ps[qt % 2][:, h * 512:(h + 1) * 512], lhs,
                                        v_buf[sp][:, kc * 1032 + h * 512: kc * 1032 + (h + 1) * 512],
                                        start=(kc == 0), stop=(kc == kcmax))
                                ins = tensor.matmul(
                                    r_ps[:, qt:qt + 1], lhs,
                                    v_buf[sp][:, kc * 1032 + 1024: kc * 1032 + 1025],
                                    start=(kc == 0), stop=(kc == kcmax))
                            ins.then_inc(sem["av_done"], 1)
                            gav += 1

                @blk.scalar
                def _(scalar):
                    scalar.wait_ge(sem["ph1"], 5)
                    scalar.dma_start(
                        out=qt_buf[0][:].rearrange("p (c s) -> p c s", c=8),
                        in_=qT_h[:, :, 0:D].rearrange("c p s -> p c s")
                    ).then_inc(qtS[0], 16)
                    for si, sup in enumerate(SUPERS):
                        if si >= 1:
                            scalar.wait_ge(sem["av_done"], 8 * si)
                        for (kc, lo, hi, idx) in sup["steps"]:
                            scalar.wait_ge(sem["sc_done"], idx + 1)
                            scalar.activation(
                                pt_buf[:, kc * D + lo: kc * D + hi],
                                sc_ps[idx % 2][:, 0:hi - lo], AF.Exp,
                                bias=cpool[:, 2:3], scale=1.0 / 32.0
                            ).then_inc(sem["exp_done"], 1)

                @blk.vector
                def _(vector):
                    vector.wait_ge(sem["ph1"], 5)
                    for p in range(2):
                        vector.memset(v_buf[p][:], 1.0).then_inc(sem["p2i"], 1)
                    o_cnt = 0
                    r_cnt = 0
                    for si, sup in enumerate(SUPERS):
                        qg, ks, diag = sup["qg"], sup["ks"], sup["diag"]
                        qp = qg % 2
                        if diag:
                            for kc in range(8):
                                vector.wait_ge(sem["exp_done"], sup["first"][kc] + 1)
                                col = kc * D + kc * 128
                                vector.tensor_mul(pt_buf[:, col:col + 128],
                                                  pt_buf[:, col:col + 128],
                                                  dmask_sb[:])\
                                    .then_inc(sem["vmask"], 1)
                        for qt in range(8):
                            vector.wait_ge(sem["av_done"], 8 * si + qt + 1)
                            if o_cnt:
                                vector.wait_ge(sem["o_add"], o_cnt)
                            dst = o_acc[qp][:, qt * D:(qt + 1) * D]
                            if ks == 0:
                                if qg >= 2 and qt == 0:
                                    vector.wait_ge(sem["vf"], 24 * (qg - 1))
                                vector.tensor_copy(dst, av_ps[qt % 2][:])\
                                    .then_inc(sem["o_add"], 1)
                            else:
                                vector.tensor_add(dst, dst, av_ps[qt % 2][:])\
                                    .then_inc(sem["o_add"], 1)
                            o_cnt += 1
                        vector.wait_ge(sem["av_done"], 8 * (si + 1))
                        if r_cnt:
                            vector.wait_ge(sem["r_evac"], r_cnt)
                        if ks == 0:
                            if qg >= 2:
                                vector.wait_ge(sem["rv"], qg - 1)
                            vector.tensor_copy(r_acc[qp][:], r_ps[:, 0:8])\
                                .then_inc(sem["r_evac"], 1)
                        else:
                            vector.tensor_add(r_acc[qp][:], r_acc[qp][:], r_ps[:, 0:8])\
                                .then_inc(sem["r_evac"], 1)
                        r_cnt += 1
                        if diag:
                            vector.wait_ge(sem["r_evac"], r_cnt)
                            vector.reciprocal(rinv[qp][:], r_acc[qp][:])\
                                .then_inc(sem["rv"], 1)
                        if diag and qg == QG - 1:
                            n_adds = 8 * (qg * (qg + 1) // 2 + qg + 1)
                            vf2c = 0
                            guse = 0
                            for qt in range(1, 8, 2):
                                guse += 1
                                rows = (qg * 8 + qt) * 128
                                vector.wait_ge(sem["o_add"], n_adds)
                                vector.wait_ge(sem["rv"], qg + 1)
                                vector.wait_ge(gS2[1], 16 * (4 * qg + guse))
                                vector.wait_ge(xrS[1], 16 * (4 * qg + guse))
                                vector.wait_ge(obS[1], 16 * (qg * 4 + qt // 2))
                                if vf2c:
                                    vector.wait_ge(sem["vf2"], vf2c)
                                vector.tensor_mul(out_sb[1][:],
                                                  o_acc[qp][:, qt * D:(qt + 1) * D],
                                                  g_buf[1][:]).then_inc(sem["vf2"], 1)
                                vf2c += 1
                                vector.wait_ge(sem["vf2"], vf2c)
                                rb = rinv[qp][:, qt:qt + 1].broadcast_to([128, D])
                                vector.tensor_mul(out_sb[1][:], out_sb[1][:], rb)\
                                    .then_inc(sem["vf2"], 1)
                                vf2c += 1
                                vector.wait_ge(sem["vf2"], vf2c)
                                vector.tensor_add(out_sb[1][:], out_sb[1][:],
                                                  x2_buf[1][:]).then_inc(sem["vf2"], 1)
                                vf2c += 1

                @blk.gpsimd
                def _(gpsimd):
                    gpsimd.wait_ge(sem["ph1"], 5)
                    gpsimd.wait_ge(sem["p2i"], 2)
                    gpsimd.dma_start(
                        out=v_buf[0][:].rearrange("p (t z) -> p t z", t=8)[:, :, 0:D],
                        in_=v_h[0:D, :].rearrange("(t p) d -> p t d", p=128)
                    ).then_inc(sem["vg0"], 16)
                    g_cnt = [0, 0]
                    ob_cnt = [0, 0]
                    vf_cnt = 0
                    for qg in range(QG):
                        qp = qg % 2
                        vf_base = qg * 24
                        n_adds = 8 * (qg * (qg + 1) // 2 + qg + 1)
                        for qt in range(8):
                            rows = (qg * 8 + qt) * 128
                            par = qt % 2
                            if qt < 2:
                                if g_cnt[par] > 0:
                                    gpsimd.wait_ge(sem["vf"],
                                                   vf_base - 24 + 3 * (6 + qt) + 3)
                                gpsimd.dma_start(out=g_buf[par][:],
                                                 in_=g_h[rows:rows + 128, :])\
                                    .then_inc(gS2[par], 16)
                                gpsimd.dma_start(out=x2_buf[par][:],
                                                 in_=x_d[rows:rows + 128, :])\
                                    .then_inc(xrS[par], 16)
                                g_cnt[par] += 1
                            split = (qg == QG - 1 and par == 1)
                            if not split:
                                gpsimd.wait_ge(sem["o_add"], n_adds)
                                gpsimd.wait_ge(sem["rv"], qg + 1)
                                gpsimd.wait_ge(gS2[par], 16 * g_cnt[par])
                                gpsimd.wait_ge(xrS[par], 16 * g_cnt[par])
                                if qg > 0 or qt >= 2:
                                    gpsimd.wait_ge(obS[par], 16 * (qg * 4 + qt // 2))
                                if vf_cnt:
                                    gpsimd.wait_ge(sem["vf"], vf_cnt)
                                gpsimd.tensor_mul(res_sb[:],
                                                  o_acc[qp][:, qt * D:(qt + 1) * D],
                                                  g_buf[par][:]).then_inc(sem["vf"], 1)
                                vf_cnt += 1
                                gpsimd.wait_ge(sem["vf"], vf_cnt)
                                rb = rinv[qp][:, qt:qt + 1].broadcast_to([128, D])
                                gpsimd.tensor_mul(res_sb[:], res_sb[:], rb)\
                                    .then_inc(sem["vf"], 1)
                                vf_cnt += 1
                                gpsimd.wait_ge(sem["vf"], vf_cnt)
                                gpsimd.tensor_add(out_sb[par][:], res_sb[:],
                                                  x2_buf[par][:]).then_inc(sem["vf"], 1)
                                vf_cnt += 1
                                gpsimd.wait_ge(sem["vf"], vf_cnt)
                            else:
                                gpsimd.wait_ge(sem["vf2"], 3 * ((qt + 1) // 2))
                            gpsimd.dma_start(out=out_d[rows:rows + 128, :],
                                             in_=out_sb[par][:])\
                                .then_inc(obS[par], 16)
                            ob_cnt[par] += 1
                            if qt + 2 < 8:
                                rows2 = (qg * 8 + qt + 2) * 128
                                if split:
                                    gpsimd.wait_ge(sem["vf2"], 3 * ((qt + 1) // 2))
                                gpsimd.dma_start(out=g_buf[par][:],
                                                 in_=g_h[rows2:rows2 + 128, :])\
                                    .then_inc(gS2[par], 16)
                                gpsimd.dma_start(out=x2_buf[par][:],
                                                 in_=x_d[rows2:rows2 + 128, :])\
                                    .then_inc(xrS[par], 16)
                                g_cnt[par] += 1
                    gpsimd.wait_ge(obS[0], 16 * ob_cnt[0])
                    gpsimd.wait_ge(obS[1], 16 * ob_cnt[1])

    return nc


def _consts():
    i = np.arange(D, dtype=np.float64)
    ang = 2.0 * np.pi * np.outer(i, i) / D
    ang = np.mod(ang, 2.0 * np.pi)
    cos = np.cos(ang)
    sin = np.sin(ang)
    return dict(
        cosm=cos.astype(BF),
        sinm=sin.astype(BF),
        sinnm=(-sin).astype(BF),
        ccm=(cos / D).astype(BF),
        csnm=(-sin / D).astype(BF),
        ident=np.eye(128, dtype=BF),
        dmask=(np.tril(np.ones((128, 128), np.float32)).T).astype(BF),
    )


def kernel(L0_attn_x, L0_attn_W_q, L0_attn_W_k, L0_attn_W_v,
           L0_ffn_w_freq_real, L0_ffn_w_freq_imag, L0_ffn_bias):
    S = 8192
    nc = build_nc(S)
    consts = _consts()
    shared = dict(
        wq=np.ascontiguousarray(L0_attn_W_q, dtype=np.float32),
        wk=np.ascontiguousarray(L0_attn_W_k, dtype=np.float32),
        wv=np.ascontiguousarray(L0_attn_W_v, dtype=np.float32),
        wfr=np.ascontiguousarray(L0_ffn_w_freq_real, dtype=np.float32),
        wfi=np.ascontiguousarray(L0_ffn_w_freq_imag, dtype=np.float32),
        fbias=np.ascontiguousarray(L0_ffn_bias, dtype=np.float32).reshape(1, D),
        **consts,
    )
    x = np.asarray(L0_attn_x, dtype=np.float32)
    in_maps = []
    for b in range(8):
        m = dict(x=np.ascontiguousarray(x[b]), **shared)
        in_maps.append(m)
    res = run_bass_kernel_spmd(nc, in_maps, core_ids=list(range(8)))
    out = np.stack([np.asarray(res.results[b]["out"]) for b in range(8)], axis=0)
    return out.astype(np.float32)
